# version 05
# baseline (speedup 1.0000x reference)
import numpy as np
import ml_dtypes
from contextlib import ExitStack

import concourse.bass as bass
import concourse.mybir as mybir
from concourse.bass_utils import run_bass_kernel_spmd

F32 = mybir.dt.float32
BF16 = mybir.dt.bfloat16
AF = mybir.ActivationFunctionType
ALU = mybir.AluOpType
AX = mybir.AxisListType

H = 1024
L = 2048
P = 256
C = 128
NCH = L // C
HC = 8
NTT = L // 128
E = 16
I_E = 1408
IC = 11
NEX = 18
HALF = 1024


def build_nc(dbg=False, lim=99, nexlim=NEX):
    nc = bass.Bass("TRN2", target_bir_lowering=False)
    dbg_drams = {}

    x_d = nc.dram_tensor("x", [L, H], F32, kind="ExternalInput")
    out_d = nc.dram_tensor("out", [L, H], F32, kind="ExternalOutput")
    bT_d = nc.dram_tensor("bT", [H, 2 * P], F32, kind="ExternalInput")
    cT_d = nc.dram_tensor("cT", [2 * P, H], F32, kind="ExternalInput")
    gw_d = nc.dram_tensor("gw", [16, H], F32, kind="ExternalInput")
    gbias_d = nc.dram_tensor("gbias", [1, 16], F32, kind="ExternalInput")
    dvec_d = nc.dram_tensor("dvec", [H], F32, kind="ExternalInput")
    steps_d = nc.dram_tensor("steps_raw", [P], F32, kind="ExternalInput")
    graw_d = nc.dram_tensor("g_raw", [P], F32, kind="ExternalInput")
    araw_d = nc.dram_tensor("a_raw", [P], F32, kind="ExternalInput")
    wgT_d = nc.dram_tensor("wgT", [E, IC, H, 128], F32, kind="ExternalInput")
    wuT_d = nc.dram_tensor("wuT", [E, IC, H, 128], F32, kind="ExternalInput")
    wdT_d = nc.dram_tensor("wdT", [E, I_E, H], F32, kind="ExternalInput")
    sgT_d = nc.dram_tensor("sgT", [2 * IC, H, 128], F32, kind="ExternalInput")
    suT_d = nc.dram_tensor("suT", [2 * IC, H, 128], F32, kind="ExternalInput")
    sdT_d = nc.dram_tensor("sdT", [2 * I_E, H], F32, kind="ExternalInput")
    tril_d = nc.dram_tensor("tril", [128, 128], F32, kind="ExternalInput")
    identf_d = nc.dram_tensor("identf", [128, 128], F32, kind="ExternalInput")
    identb_d = nc.dram_tensor("identb", [128, 128], BF16, kind="ExternalInput")
    ones_d = nc.dram_tensor("onesr", [1, 128], F32, kind="ExternalInput")

    es = ExitStack()
    es01 = ExitStack()
    es0 = ExitStack()
    es1 = ExitStack()
    esF = ExitStack()

    def mk_sb(stack):
        return lambda name, shape, dt: stack.enter_context(
            nc.sbuf_tensor(name, shape, dt))

    sb = mk_sb(es)
    sb01 = mk_sb(es01)
    sb0 = mk_sb(es0)
    sb1 = mk_sb(es1)
    sbF = mk_sb(esF)
    sem = lambda name: es.enter_context(nc.semaphore(name))

    xnb = sb("xnb", [128, HC * L], BF16)
    s_sb = sb("s_sb", [128, HC * L], BF16)
    identf_sb = sb("identf_sb", [128, 128], F32)
    identb_sb = sb("identb_sb", [128, 128], BF16)
    ones_sb = sb("ones_sb", [1, 128], F32)
    gwT_sb = sb("gwT_sb", [128, 128], F32)
    gbias_sb = sb("gbias_sb", [1, 16], F32)
    bias_bc = sb("bias_bc", [128, 16], F32)
    dcol = sb("dcol", [128, HC], F32)
    prm = sb("prm", [128, 2], F32)
    graw_sb = sb("graw_sb", [128, 2], F32)
    araw_sb = sb("araw_sb", [128, 2], F32)
    PRM = {}
    for nm in ("steps", "Sv", "rS", "alpha", "m12", "m21", "m22", "ap2", "det",
               "beta", "rbeta", "n11", "n12", "n21", "n22", "sig2", "bp", "bq",
               "mc11", "mc12", "mc21", "mc22"):
        PRM[nm] = sb("prm_" + nm, [128, 2], F32)
    scr1 = sb("scr1", [128, 2], F32)
    scr2 = sb("scr2", [128, 2], F32)
    eps_col = sb("eps_col", [128, 1], F32)
    mpq = sb("mpq", [128, 4], F32)
    dscr = sb("dscr", [128, 130], F32)
    dscr2 = sb("dscr2", [128, 130], F32)
    r_col = sb("r_col", [128, 6], F32)
    cwT = sb("cwT", [16, L], F32)

    tril_sb = sb01("tril_sb", [128, 128], F32)
    Pm_c = [sb01(f"Pm_c{p_}", [128, C + 1], F32) for p_ in range(2)]
    Qm_c = [sb01(f"Qm_c{p_}", [128, C + 1], F32) for p_ in range(2)]
    K21_c = [sb01(f"K21_c{p_}", [128, C], F32) for p_ in range(2)]
    K22_c = [sb01(f"K22_c{p_}", [128, C], F32) for p_ in range(2)]
    W_km = sb01("W_km", [128, 4 * 512], F32)
    B_bf = sb01("B_bf", [128, HC * 512], BF16)
    C_bf = sb01("C_bf", [128, 4 * H], BF16)

    Pt_c = [sb0(f"Pt_c{p_}", [128, C], F32) for p_ in range(2)]
    Qt_c = [sb0(f"Qt_c{p_}", [128, C], F32) for p_ in range(2)]
    wcm = sb0("wcm", [128, 128], F32)
    x_tok = [sb0(f"x_tok{i}", [128, H], F32) for i in range(2)]
    xn_tok = sb0("xn_tok", [128, H], F32)
    sq_tok = sb0("sq_tok", [128, H], F32)
    xn_f = sb0("xn_f", [128, H], F32)
    stat = sb0("stat", [128, 8], F32)
    sd_t = sb0("sd_t", [128, 1], F32)
    logits = sb0("logits", [128, NTT * 16], F32)
    scores = sb0("scores", [128, NTT * 16], F32)
    r_sfc = sb0("r_sfc", [128, NTT * 16], F32)
    r_eq = sb0("r_eq", [128, NTT * 16], F32)
    r_tmp = sb0("r_tmp", [128, NTT * 16], F32)
    r_w = sb0("r_w", [128, NTT * 16], F32)
    r_m64a = sb0("r_m64a", [128, 64], F32)
    r_m64b = sb0("r_m64b", [128, 64], F32)
    r_gs64 = sb0("r_gs64", [128, 64], F32)
    r_g16a = sb0("r_g16a", [128, 16], F32)
    r_g16b = sb0("r_g16b", [128, 16], F32)
    cw_sb = sb0("cw_sb", [128, NTT * 16], F32)
    st_pg = sb0("st_pg", [8, 128], F32)
    st_dv = sb0("st_dv", [8, 128], F32)
    st_gw = sb0("st_gw", [16, H], F32)
    statall = sb0("statall", [128, NTT * 8], F32)
    xdbg = sb0("xdbg", [128, 2 * H], F32)

    names = ["SXs", "SXa", "SXb", "SWbc", "SWg0", "SWg1", "SWu0", "SWu1",
             "SWd", "SCa", "SCb", "SRa", "SRb", "SOa", "SOb",
             "ST", "SV", "SS", "SVD", "SSD"]
    SEMS = {n: sem(n) for n in names}
    cnt = {n: 0 for n in names}

    ENG = ("t", "v", "s", "y", "g")
    plan = None

    def new_plan():
        return {e: [] for e in ENG}

    def run_block(p):
        with nc.Block() as block:
            @block.tensor
            def _(eng):
                for f in p["t"]:
                    f(eng)

            @block.vector
            def _(eng):
                for f in p["v"]:
                    f(eng)

            @block.scalar
            def _(eng):
                for f in p["s"]:
                    f(eng)

            @block.sync
            def _(eng):
                for f in p["y"]:
                    f(eng)

            @block.gpsimd
            def _(eng):
                for f in p["g"]:
                    f(eng)

    def wt(e, semname, val):
        if val is None or val <= 0:
            return
        s_, v_ = SEMS[semname], val
        plan[e].append(lambda eng: eng.wait_ge(s_, v_))

    def bump(semname, k):
        cnt[semname] += k
        return cnt[semname]

    def op(e, make, mark=None, inc=1):
        if e in ("v", "s"):
            if mark is None:
                mark_, inc_ = ("SVD" if e == "v" else "SSD"), 1
            else:
                mark_, inc_ = mark, inc
            s_ = SEMS[mark_]
            plan[e].append(lambda eng: make(eng).then_inc(s_, inc_))
            n_ = bump(mark_, inc_)
            plan[e].append(lambda eng: eng.wait_ge(s_, n_))
            return n_ if mark is not None else None
        if mark is None:
            plan[e].append(lambda eng: make(eng))
            return None
        s_ = SEMS[mark]
        k_ = inc
        plan[e].append(lambda eng: make(eng).then_inc(s_, k_))
        return bump(mark, inc)

    def V_(make, mark=None):
        return op("v", make, mark=mark)

    def dump(name, handle):
        if not dbg:
            return
        shp = list(handle.shape)
        d = nc.dram_tensor("dbg_" + name, shp, handle.dtype,
                           kind="ExternalOutput")
        dbg_drams[name] = d
        for s_nm in ("SV", "SS", "ST"):
            wt("y", s_nm, cnt[s_nm])
        op("y", lambda eng, d=d, h=handle: eng.dma_start(
            out=d[:, :] if len(shp) == 2 else d[:], in_=h[:, :]
            if len(shp) == 2 else h[:]), mark="SOa", inc=16)

    def opt_tr(make, mark=None):
        r = op("t", make, mark=mark)
        plan["t"].append(lambda eng: eng.drain())
        return r

    plan = new_plan()
    p0 = plan
    ps0 = ExitStack()
    psT = ps0.enter_context(nc.psum_tensor("psT", [128, 512], F32))
    ps_gate = ps0.enter_context(nc.psum_tensor("ps_gate", [128, 16], F32))
    ps_cwt = ps0.enter_context(nc.psum_tensor("ps_cwt", [16, 128], F32))
    ps_head = ExitStack()
    ps_w = ps_head.enter_context(nc.psum_tensor("ps_w", [128, 128], F32))
    ps_tr0 = ps_head.enter_context(nc.psum_tensor("ps_tr0", [128, 272], F32))
    ps_bc = ps_tr0

    for dst, src in [
        (tril_sb[:, :], tril_d[:, :]),
        (identf_sb[:, :], identf_d[:, :]),
        (identb_sb[:, :], identb_d[:, :]),
        (ones_sb[:, :], ones_d[:, :]),
        (gbias_sb[:, :], gbias_d[:, :]),
        (st_pg[0:2, :], steps_d.ap().rearrange("(pt p) -> pt p", p=128)),
        (st_pg[2:4, :], graw_d.ap().rearrange("(pt p) -> pt p", p=128)),
        (st_pg[4:6, :], araw_d.ap().rearrange("(pt p) -> pt p", p=128)),
        (st_dv[:, :], dvec_d.ap().rearrange("(hc p) -> hc p", p=128)),
        (st_gw[:, :], gw_d[:, :]),
    ]:
        op("y", lambda eng, d=dst, s_=src: eng.dma_start(out=d, in_=s_),
           mark="SXs", inc=16)
    SX_setup = cnt["SXs"]
    wt("t", "SXs", SX_setup)
    opt_tr(lambda eng: eng.transpose(ps_tr0[:, 0:8], st_pg[:, :],
                                     identf_sb[0:8, 0:8]))
    opt_tr(lambda eng: eng.transpose(ps_tr0[:, 8:16], st_dv[:, :],
                                     identf_sb[0:8, 0:8]))
    for hc in range(HC):
        opt_tr(lambda eng, hc=hc: eng.transpose(
            ps_tr0[:, 16 + hc * 16:32 + hc * 16],
            st_gw[0:16, hc * 128:(hc + 1) * 128], identf_sb[0:16, 0:16]),
            mark="ST" if hc == HC - 1 else None)
    ST_par = cnt["ST"]
    wt("v", "ST", ST_par)
    V_(lambda e: e.tensor_copy(prm[:, :], ps_tr0[:, 0:2]))
    V_(lambda e: e.tensor_copy(graw_sb[:, :], ps_tr0[:, 2:4]))
    V_(lambda e: e.tensor_copy(araw_sb[:, :], ps_tr0[:, 4:6]))
    V_(lambda e: e.tensor_copy(dcol[:, :], ps_tr0[:, 8:16]))
    SV_par = V_(lambda e: e.tensor_copy(gwT_sb[:, :], ps_tr0[:, 16:144]),
                mark="SV")

    x_sx = [None] * NTT
    xre_guard = [None] * NTT
    sv_xn = [None] * NTT
    st_tp = [None] * NTT
    sv_evac = [None] * NTT
    st_gate = [None] * NTT
    ss_sq = [None] * NTT

    def plan_x_load(tt):
        if tt >= 2:
            sv_, ss_g_ = xre_guard[tt - 2]
            wt("y", "SV", sv_)
            wt("y", "SS", ss_g_)
        s_nm = "SXa" if tt % 2 == 0 else "SXb"
        op("y", lambda eng, tt=tt: eng.dma_start(
            out=x_tok[tt % 2][:, :], in_=x_d[tt * 128:(tt + 1) * 128, :]),
           mark=s_nm, inc=16)
        x_sx[tt] = (s_nm, cnt[s_nm])

    for hc in range(HC if lim >= 4 else 0):
        op("g", lambda eng, hc=hc: eng.dma_start(
            out=B_bf[:, hc * 512:(hc + 1) * 512],
            in_=bT_d[hc * 128:(hc + 1) * 128, :]), mark="SWbc", inc=16)
    for ct in range(4 if lim >= 4 else 0):
        op("g", lambda eng, ct=ct: eng.dma_start(
            out=C_bf[:, ct * H:(ct + 1) * H],
            in_=cT_d[ct * 128:(ct + 1) * 128, :]), mark="SWbc", inc=16)
    SW_bc = cnt["SWbc"]
    wt("v", "SWbc", SW_bc)
    for ct in ((2, 3) if lim >= 4 else ()):
        V_(lambda e, ct=ct: e.tensor_scalar_mul(
            C_bf[:, ct * H:(ct + 1) * H], C_bf[:, ct * H:(ct + 1) * H], -1.0),
           mark="SV" if ct == 3 else None)

    V_(lambda e: e.memset(eps_col[:, :], 1e-5))
    wt("s", "SV", SV_par)
    op("s", lambda eng: eng.activation(PRM["steps"][:, :], prm[:, :], AF.Sigmoid),
       mark="SS")
    SS_steps = cnt["SS"]

    wt("v", "SS", SS_steps)
    st_ = PRM["steps"]
    V_(lambda e: e.tensor_scalar_max(graw_sb[:, :], graw_sb[:, :], 0.0))
    V_(lambda e: e.tensor_scalar_max(araw_sb[:, :], araw_sb[:, :], 0.0))
    V_(lambda e: e.tensor_mul(scr1[:, :], st_[:, :], graw_sb[:, :]))
    V_(lambda e: e.tensor_scalar_add(PRM["Sv"][:, :], scr1[:, :], 1.0))
    V_(lambda e: e.reciprocal(PRM["rS"][:, :], PRM["Sv"][:, :]))
    V_(lambda e: e.tensor_mul(scr1[:, :], st_[:, :], st_[:, :]))
    V_(lambda e: e.tensor_mul(scr1[:, :], scr1[:, :], araw_sb[:, :]))
    sv_a0 = V_(lambda e: e.tensor_mul(scr1[:, :], scr1[:, :], PRM["rS"][:, :]),
               mark="SV")
    wt("s", "SV", sv_a0)
    ss_tanh = op("s", lambda eng: eng.activation(
        scr2[:, :], scr1[:, :], AF.Tanh, scale=1.0 / 1.99), mark="SS")
    wt("v", "SS", ss_tanh)
    V_(lambda e: e.tensor_scalar_mul(PRM["alpha"][:, :], scr2[:, :], 1.99))
    V_(lambda e: e.reciprocal(scr1[:, :], st_[:, :]))
    V_(lambda e: e.tensor_mul(scr2[:, :], PRM["alpha"][:, :], scr1[:, :]))
    V_(lambda e: e.tensor_scalar_mul(PRM["m12"][:, :], scr2[:, :], -1.0))
    V_(lambda e: e.tensor_mul(PRM["m21"][:, :], st_[:, :], PRM["rS"][:, :]))
    V_(lambda e: e.tensor_scalar(PRM["m22"][:, :], PRM["alpha"][:, :], -1.0, 1.0,
                                 ALU.mult, ALU.add))
    V_(lambda e: e.tensor_add(scr1[:, :], PRM["rS"][:, :], PRM["m22"][:, :]))
    V_(lambda e: e.tensor_scalar_mul(PRM["ap2"][:, :], scr1[:, :], 0.5))
    V_(lambda e: e.tensor_mul(scr1[:, :], PRM["rS"][:, :], PRM["m22"][:, :]))
    V_(lambda e: e.tensor_mul(scr2[:, :], PRM["m12"][:, :], PRM["m21"][:, :]))
    V_(lambda e: e.tensor_sub(PRM["det"][:, :], scr1[:, :], scr2[:, :]))
    V_(lambda e: e.tensor_mul(scr1[:, :], PRM["ap2"][:, :], PRM["ap2"][:, :]))
    V_(lambda e: e.tensor_sub(scr2[:, :], PRM["det"][:, :], scr1[:, :]))
    sv_disc = V_(lambda e: e.tensor_scalar_max(scr2[:, :], scr2[:, :], 1e-20),
                 mark="SV")
    wt("s", "SV", sv_disc)
    ss_beta = op("s", lambda eng: eng.activation(
        PRM["beta"][:, :], scr2[:, :], AF.Sqrt), mark="SS")
    wt("v", "SS", ss_beta)
    V_(lambda e: e.reciprocal(PRM["rbeta"][:, :], PRM["beta"][:, :]))
    V_(lambda e: e.tensor_sub(scr1[:, :], PRM["rS"][:, :], PRM["ap2"][:, :]))
    V_(lambda e: e.tensor_mul(PRM["n11"][:, :], scr1[:, :], PRM["rbeta"][:, :]))
    V_(lambda e: e.tensor_mul(PRM["n12"][:, :], PRM["m12"][:, :], PRM["rbeta"][:, :]))
    V_(lambda e: e.tensor_mul(PRM["n21"][:, :], PRM["m21"][:, :], PRM["rbeta"][:, :]))
    V_(lambda e: e.tensor_sub(scr1[:, :], PRM["m22"][:, :], PRM["ap2"][:, :]))
    V_(lambda e: e.tensor_mul(PRM["n22"][:, :], scr1[:, :], PRM["rbeta"][:, :]))
    V_(lambda e: e.tensor_mul(PRM["sig2"][:, :], st_[:, :], PRM["m21"][:, :]))
    V_(lambda e: e.reciprocal(scr1[:, :], PRM["det"][:, :]))
    V_(lambda e: e.tensor_mul(PRM["bp"][:, :], PRM["ap2"][:, :], scr1[:, :]))
    V_(lambda e: e.tensor_mul(scr2[:, :], PRM["beta"][:, :], scr1[:, :]))
    V_(lambda e: e.tensor_scalar_mul(PRM["bq"][:, :], scr2[:, :], -1.0))

    def build_tables(Pt_arr, Qt_arr, re_nm, im_nm, ncols):
        for pt in range(2):
            Pm, Qm = Pt_arr[pt], Qt_arr[pt]
            zre = PRM[re_nm][:, pt:pt + 1]
            zim = PRM[im_nm][:, pt:pt + 1]
            V_(lambda e, Pm=Pm: e.memset(Pm[:, 0:1], 1.0))
            V_(lambda e, Qm=Qm: e.memset(Qm[:, 0:1], 0.0))
            V_(lambda e, Pm=Pm, zre=zre: e.tensor_copy(Pm[:, 1:2], zre))
            V_(lambda e, Qm=Qm, zim=zim: e.tensor_copy(Qm[:, 1:2], zim))
            c = 2
            while c < ncols:
                V_(lambda e, Pm=Pm, zre=zre, c=c: e.tensor_scalar_mul(
                    mpq[:, 2:3], Pm[:, c - 1:c], zre))
                V_(lambda e, Qm=Qm, zim=zim, c=c: e.tensor_scalar_mul(
                    mpq[:, 3:4], Qm[:, c - 1:c], zim))
                V_(lambda e: e.tensor_sub(mpq[:, 0:1], mpq[:, 2:3], mpq[:, 3:4]))
                V_(lambda e, Pm=Pm, zim=zim, c=c: e.tensor_scalar_mul(
                    mpq[:, 2:3], Pm[:, c - 1:c], zim))
                V_(lambda e, Qm=Qm, zre=zre, c=c: e.tensor_scalar_mul(
                    mpq[:, 3:4], Qm[:, c - 1:c], zre))
                V_(lambda e: e.tensor_add(mpq[:, 1:2], mpq[:, 2:3], mpq[:, 3:4]))
                n = min(c, ncols - c)
                V_(lambda e, Pm=Pm, n=n: e.tensor_scalar_mul(
                    dscr[:, 0:n], Pm[:, 0:n], mpq[:, 0:1]))
                V_(lambda e, Qm=Qm, n=n: e.tensor_scalar_mul(
                    dscr2[:, 0:n], Qm[:, 0:n], mpq[:, 1:2]))
                V_(lambda e, Pm=Pm, c=c, n=n: e.tensor_sub(
                    Pm[:, c:c + n], dscr[:, 0:n], dscr2[:, 0:n]))
                V_(lambda e, Pm=Pm, n=n: e.tensor_scalar_mul(
                    dscr[:, 0:n], Pm[:, 0:n], mpq[:, 1:2]))
                V_(lambda e, Qm=Qm, n=n: e.tensor_scalar_mul(
                    dscr2[:, 0:n], Qm[:, 0:n], mpq[:, 0:1]))
                V_(lambda e, Qm=Qm, c=c, n=n: e.tensor_add(
                    Qm[:, c:c + n], dscr[:, 0:n], dscr2[:, 0:n]))
                c *= 2

    if lim >= 1:
        build_tables(Pm_c, Qm_c, "ap2", "beta", C + 1)
        build_tables(Pt_c, Qt_c, "bp", "bq", C)

    for pt in range(2 if lim >= 1 else 0):
        V_(lambda e, pt=pt: e.tensor_scalar_mul(
            K21_c[pt][:, :], Qm_c[pt][:, 1:C + 1], PRM["n21"][:, pt:pt + 1]))
        V_(lambda e, pt=pt: e.tensor_scalar_mul(
            dscr[:, 0:C], Qm_c[pt][:, 1:C + 1], PRM["n22"][:, pt:pt + 1]))
        V_(lambda e, pt=pt: e.tensor_add(
            K22_c[pt][:, :], Pm_c[pt][:, 1:C + 1], dscr[:, 0:C]))
        V_(lambda e, pt=pt: e.tensor_scalar_mul(
            scr1[:, pt:pt + 1], Qm_c[pt][:, C:C + 1], PRM["n11"][:, pt:pt + 1]))
        V_(lambda e, pt=pt: e.tensor_add(
            PRM["mc11"][:, pt:pt + 1], Pm_c[pt][:, C:C + 1], scr1[:, pt:pt + 1]))
        V_(lambda e, pt=pt: e.tensor_scalar_mul(
            PRM["mc12"][:, pt:pt + 1], Qm_c[pt][:, C:C + 1], PRM["n12"][:, pt:pt + 1]))
        V_(lambda e, pt=pt: e.tensor_copy(
            PRM["mc21"][:, pt:pt + 1], K21_c[pt][:, C - 1:C]))
        V_(lambda e, pt=pt: e.tensor_copy(
            PRM["mc22"][:, pt:pt + 1], K22_c[pt][:, C - 1:C]))

    prev_tr = None
    for mi in range(4 if lim >= 1 else 0):
        for pt in range(2):
            Ptt, Qtt = Pt_c[pt], Qt_c[pt]
            s1c = PRM["m21"][:, pt:pt + 1]
            s2c = PRM["sig2"][:, pt:pt + 1]
            n1c = PRM["n11" if mi < 2 else "n21"][:, pt:pt + 1]
            n2c = PRM["n12" if mi < 2 else "n22"][:, pt:pt + 1]
            flip = (mi % 2 == 1)
            if mi < 2:
                base, other = (Qtt, Ptt) if not flip else (Ptt, Qtt)
                V_(lambda e, base=base, n1c=n1c: e.tensor_scalar_mul(
                    dscr[:, 0:C], base[:, :], n1c))
                if not flip:
                    V_(lambda e, other=other: e.tensor_add(
                        dscr[:, 0:C], other[:, :], dscr[:, 0:C]))
                else:
                    V_(lambda e, other=other: e.tensor_sub(
                        dscr[:, 0:C], dscr[:, 0:C], other[:, :]))
                V_(lambda e, s1c=s1c: e.tensor_scalar_mul(
                    dscr[:, 0:C], dscr[:, 0:C], s1c))
                src2 = Qtt if not flip else Ptt
                V_(lambda e, src2=src2, n2c=n2c: e.tensor_scalar_mul(
                    dscr2[:, 0:C], src2[:, :], n2c))
                V_(lambda e, s2c=s2c: e.tensor_scalar_mul(
                    dscr2[:, 0:C], dscr2[:, 0:C], s2c))
            else:
                src1 = Qtt if not flip else Ptt
                V_(lambda e, src1=src1, n1c=n1c: e.tensor_scalar_mul(
                    dscr[:, 0:C], src1[:, :], n1c))
                V_(lambda e, s1c=s1c: e.tensor_scalar_mul(
                    dscr[:, 0:C], dscr[:, 0:C], s1c))
                base, other = (Qtt, Ptt) if not flip else (Ptt, Qtt)
                V_(lambda e, base=base, n2c=n2c: e.tensor_scalar_mul(
                    dscr2[:, 0:C], base[:, :], n2c))
                if not flip:
                    V_(lambda e, other=other: e.tensor_add(
                        dscr2[:, 0:C], other[:, :], dscr2[:, 0:C]))
                else:
                    V_(lambda e, other=other: e.tensor_sub(
                        dscr2[:, 0:C], dscr2[:, 0:C], other[:, :]))
                V_(lambda e, s2c=s2c: e.tensor_scalar_mul(
                    dscr2[:, 0:C], dscr2[:, 0:C], s2c))
            if prev_tr is not None:
                wt("v", "ST", prev_tr)
            sv_w = V_(lambda e: e.tensor_add(wcm[:, :], dscr[:, 0:C],
                                             dscr2[:, 0:C]), mark="SV")
            wt("t", "SV", sv_w)
            prev_tr = opt_tr(lambda eng: eng.transpose(
                ps_w[:, :], wcm[:, :], identf_sb[:, :]), mark="ST")
            wt("v", "ST", prev_tr)
            for j in range(2):
                col = mi * 512 + j * 256 + pt * 128
                V_(lambda e, col=col: e.tensor_copy(
                    W_km[:, col:col + 128], ps_w[:, :]),
                   mark="SV" if j == 1 else None)

    if lim < 1:
        if dbg:
            for nm_ in PRM:
                dump("prm_" + nm_, PRM[nm_])
            for pt_ in range(2):
                dump(f"Pm_c{pt_}", Pm_c[pt_])
                dump(f"Qm_c{pt_}", Qm_c[pt_])
        run_block(p0)
        ps0.close()
        es0.close()
        es01.close()
        plan = new_plan()
        wt("y", "SOa", cnt["SOa"])
        run_block(plan)
        esF.close()
        es.close()
        return nc
    wt("t", "SXs", SX_setup)
    st_bias = op("t", lambda eng: eng.matmul(
        ps_tr0[:, 256:272], ones_sb[:, :], gbias_sb[:, :], start=True,
        stop=True),
        mark="ST")
    wt("v", "ST", st_bias)
    V_(lambda e: e.tensor_copy(bias_bc[:, :], ps_tr0[:, 256:272]))

    if lim >= 2:
        plan_x_load(0)
        plan_x_load(1)
    for tt in range(NTT if lim >= 3 else 0):
        par = tt % 2
        wt("s", x_sx[tt][0], x_sx[tt][1])
        if tt >= 1:
            wt("s", "SV", sv_xn[tt - 1])
        op("s", lambda eng, par=par: eng.activation(
            sq_tok[:, :], x_tok[par][:, :], AF.Square,
            accum_out=stat[:, 1:2]), mark="SS")
        ss_sq[tt] = op("s", lambda eng, par=par: eng.activation(
            sq_tok[:, :], x_tok[par][:, :], AF.Copy,
            accum_out=stat[:, 0:1]), mark="SS")
        wt("v", x_sx[tt][0], x_sx[tt][1])
        wt("v", "SS", ss_sq[tt])
        if dbg and tt < 2:
            V_(lambda e, par=par, tt=tt: e.tensor_copy(
                xdbg[:, tt * H:(tt + 1) * H], x_tok[par][:, :]))
        if tt >= 1:
            wt("v", "ST", st_tp[tt - 1])
        V_(lambda e: e.tensor_scalar_mul(stat[:, 0:1], stat[:, 0:1], 1.0 / H))
        V_(lambda e: e.tensor_mul(stat[:, 2:3], stat[:, 0:1], stat[:, 0:1]))
        sv_var = V_(lambda e: e.scalar_tensor_tensor(
            stat[:, 3:4], stat[:, 1:2], 1.0 / H, stat[:, 2:3],
            ALU.mult, ALU.subtract), mark="SV")
        wt("s", "SV", sv_var)
        ss_sd = op("s", lambda eng: eng.activation(sd_t[:, :], stat[:, 3:4],
                                                   AF.Sqrt,
                                                   bias=eps_col[:, :]),
                   mark="SS")
        wt("v", "SS", ss_sd)
        V_(lambda e: e.reciprocal(stat[:, 5:6], sd_t[:, :]))
        sv_xn[tt] = V_(lambda e, par=par: e.tensor_scalar(
            xn_tok[:, :], x_tok[par][:, :], stat[:, 0:1], stat[:, 5:6],
            ALU.subtract, ALU.mult), mark="SV")
        if dbg:
            V_(lambda e, tt=tt: e.tensor_copy(
                statall[:, tt * 8:(tt + 1) * 8], stat[:, :]))
        wt("t", "SV", sv_xn[tt])
        for rnd in range(2):
            if tt >= 1 or rnd == 1:
                wt("t", "SV", sv_evac[tt] if rnd == 1 else sv_evac[tt - 1])
            for hq in range(4):
                opt_tr(lambda eng, rnd=rnd, hq=hq: eng.transpose(
                    psT[:, hq * 128:(hq + 1) * 128],
                    xn_tok[:, (rnd * 4 + hq) * 128:(rnd * 4 + hq + 1) * 128],
                    identf_sb[:, :]),
                    mark="ST" if hq == 3 else None)
            st_tp[tt] = cnt["ST"]
            wt("v", "ST", st_tp[tt])
            V_(lambda e, rnd=rnd: e.tensor_copy(
                xn_f[:, rnd * 512:(rnd + 1) * 512], psT[:, :]))
            for hq in range(4):
                hc = rnd * 4 + hq
                mk = "SV" if hq == 3 else None
                sv_evac[tt] = V_(lambda e, tt=tt, hc=hc: e.tensor_copy(
                    xnb[:, hc * L + tt * 128: hc * L + (tt + 1) * 128],
                    xn_f[:, hc * 128:(hc + 1) * 128]), mark=mk)
        wt("t", "SV", sv_evac[tt])
        for hc in range(HC):
            op("t", lambda eng, hc=hc: eng.matmul(
                ps_gate[:, :], xn_f[:, hc * 128:(hc + 1) * 128],
                gwT_sb[:, hc * 16:(hc + 1) * 16],
                start=(hc == 0), stop=(hc == HC - 1)),
               mark="ST" if hc == HC - 1 else None)
        st_gate[tt] = cnt["ST"]
        wt("v", "ST", st_gate[tt])
        sv_logits_last = V_(lambda e, tt=tt: e.tensor_copy(
            logits[:, tt * 16:(tt + 1) * 16], ps_gate[:, :]), mark="SV")
        xre_guard[tt] = (sv_logits_last, ss_sq[tt])
        if tt + 2 < NTT:
            plan_x_load(tt + 2)

    if lim < 3:
        sv_logits_last = None
    wt("s", "SV", sv_logits_last)
    SS_scores = (op("s", lambda eng: eng.activation(scores[:, :], logits[:, :],
                                                    AF.Sigmoid), mark="SS")
                 if lim >= 3 else None)
    wt("v", "SS", SS_scores)
    if lim >= 3:
        def g4(t):
            return t.ap().rearrange("p (g f) -> p g f", f=4)

        def g16(t):
            return t.ap().rearrange("p (g f) -> p g f", f=16)

        def bc4(t):
            return bass.AP(t, 0, [[64, 128], [1, 64], [0, 4]])

        def bc16(t):
            return bass.AP(t, 0, [[16, 128], [1, 16], [0, 16]])

        bias_bcast = bass.AP(bias_bc, 0, [[16, 128], [0, NTT], [1, 16]])
        V_(lambda e: e.tensor_add(
            r_sfc.ap().rearrange("p (t e) -> p t e", e=16),
            scores.ap().rearrange("p (t e) -> p t e", e=16), bias_bcast))
        V_(lambda e: e.reduce_max(g4(r_m64a), g4(r_sfc), axis=AX.X))
        V_(lambda e: e.tensor_tensor(g4(r_eq), g4(r_sfc), bc4(r_m64a),
                                     ALU.is_equal))
        V_(lambda e: e.scalar_tensor_tensor(r_eq[:, :], r_eq[:, :],
                                            -1e30, r_sfc[:, :],
                                            ALU.mult, ALU.add))
        V_(lambda e: e.reduce_max(g4(r_m64b), g4(r_eq), axis=AX.X))
        V_(lambda e: e.tensor_add(r_gs64[:, :], r_m64a[:, :], r_m64b[:, :]))
        V_(lambda e: e.reduce_max(
            r_g16a.ap().rearrange("p (t o) -> p t o", o=1),
            r_gs64.ap().rearrange("p (t g) -> p t g", g=4), axis=AX.X))
        V_(lambda e: e.tensor_tensor(
            r_m64b.ap().rearrange("p (t g) -> p t g", g=4),
            r_gs64.ap().rearrange("p (t g) -> p t g", g=4),
            bass.AP(r_g16a, 0, [[16, 128], [1, 16], [0, 4]]), ALU.is_equal))
        V_(lambda e: e.scalar_tensor_tensor(r_m64b[:, :], r_m64b[:, :],
                                            -1e30, r_gs64[:, :],
                                            ALU.mult, ALU.add))
        V_(lambda e: e.reduce_max(
            r_g16b.ap().rearrange("p (t o) -> p t o", o=1),
            r_m64b.ap().rearrange("p (t g) -> p t g", g=4), axis=AX.X))
        V_(lambda e: e.tensor_tensor(
            r_m64a.ap().rearrange("p (t g) -> p t g", g=4),
            r_gs64.ap().rearrange("p (t g) -> p t g", g=4),
            bass.AP(r_g16b, 0, [[16, 128], [1, 16], [0, 4]]), ALU.is_ge))
        V_(lambda e: e.tensor_tensor(g4(r_tmp), g4(r_sfc), bc4(r_m64a),
                                     ALU.mult))
        V_(lambda e: e.tensor_copy(r_w[:, :], r_tmp[:, :]))
        V_(lambda e: e.reduce_max(
            r_g16a.ap().rearrange("p (t o) -> p t o", o=1), g16(r_w),
            axis=AX.X))
        for _ in range(3):
            V_(lambda e: e.tensor_tensor(g16(r_eq), g16(r_w), bc16(r_g16a),
                                         ALU.is_equal))
            V_(lambda e: e.scalar_tensor_tensor(r_w[:, :], r_eq[:, :],
                                                -1e30, r_w[:, :],
                                                ALU.mult, ALU.add))
            V_(lambda e: e.reduce_max(
                r_g16a.ap().rearrange("p (t o) -> p t o", o=1), g16(r_w),
                axis=AX.X))
        V_(lambda e: e.tensor_tensor(g16(r_eq), g16(r_tmp), bc16(r_g16a),
                                     ALU.is_ge))
        V_(lambda e: e.tensor_mul(r_w[:, :], scores[:, :], r_eq[:, :]))
        V_(lambda e: e.reduce_sum(
            r_g16a.ap().rearrange("p (t o) -> p t o", o=1), g16(r_w),
            axis=AX.X))
        V_(lambda e: e.tensor_scalar_add(r_g16a[:, :], r_g16a[:, :], 1e-20))
        V_(lambda e: e.reciprocal(r_g16b[:, :], r_g16a[:, :]))
        sv_cwall = V_(lambda e: e.tensor_tensor(g16(cw_sb), g16(r_w),
                                                bc16(r_g16b), ALU.mult),
                      mark="SV")
        prev_cp = None
        for tt in range(NTT):
            sl = slice(tt * 16, (tt + 1) * 16)
            wt("t", "SV", sv_cwall)
            if prev_cp:
                wt("t", "SV", prev_cp)
            st_cwt = opt_tr(lambda eng, sl=sl: eng.transpose(
                ps_cwt[:, :], cw_sb[:, sl], identf_sb[:, :]), mark="ST")
            wt("v", "ST", st_cwt)
            prev_cp = V_(lambda e, tt=tt: e.tensor_copy(
                cwT[0:16, tt * 128:(tt + 1) * 128], ps_cwt[:, :]), mark="SV")

    if dbg:
        dump("xdbg", xdbg)
        dump("statall", statall)
        dump("xnb", xnb)
        dump("logits", logits)
        dump("scores", scores)
        dump("cw_sb", cw_sb)
        dump("cwT", cwT)
        dump("W_km", W_km)
        for pt_ in range(2):
            dump(f"Pm_c{pt_}", Pm_c[pt_])
            dump(f"Qm_c{pt_}", Qm_c[pt_])
            dump(f"Pt_c{pt_}", Pt_c[pt_])
            dump(f"Qt_c{pt_}", Qt_c[pt_])
            dump(f"K21_c{pt_}", K21_c[pt_])
            dump(f"K22_c{pt_}", K22_c[pt_])
        for nm_ in PRM:
            dump("prm_" + nm_, PRM[nm_])
        dump("prm_raw", prm)
        dump("graw", graw_sb)
        dump("araw", araw_sb)
        dump("dcol", dcol)
        dump("gwT_sb", gwT_sb)
        dump("bias_bc", bias_bc)
        dump("B_bf", B_bf)
        dump("C_bf", C_bf)
    ps_head.close()
    G_k = sb1("G_k", [128, 4 * 512], F32)
    x2l = sb1("x2l", [128, 4 * L], BF16)
    x2g = x2l
    xl1 = sb1("xl1", [128, 4 * NCH], F32)
    S1_sb = sb1("S1_sb", [128, 4 * (NCH + 1)], F32)
    S2_sb = sb1("S2_sb", [128, 4 * (NCH + 1)], F32)
    ps1 = ExitStack()
    ps_bu = ps1.enter_context(nc.psum_tensor("ps_bu", [128, 512], F32))
    ps_csa = ps1.enter_context(nc.psum_tensor("ps_csa", [128, 512], F32))
    ps_csb = ps1.enter_context(nc.psum_tensor("ps_csb", [128, 512], F32))
    ps_cs1 = ps1.enter_context(nc.psum_tensor("ps_cs1", [128, 8], F32))
    ps_cy = [ps1.enter_context(nc.psum_tensor("ps_cy0", [128, 512], F32))] * 2
    if lim >= 4:
        wt("t", "SWbc", SW_bc)
        if sv_evac[NTT - 1]:
            wt("t", "SV", sv_evac[NTT - 1])

    sv_g = [None] * NCH
    st_bu = [None] * NCH
    st_cs = [None] * NCH
    sv_pm = [None] * NCH
    for ci in range(NCH if lim >= 4 else 0):
        if ci >= 1:
            wt("t", "SV", sv_g[ci - 1])
        for hc in range(HC):
            op("t", lambda eng, ci=ci, hc=hc: eng.matmul(
                ps_bu[:, :],
                xnb[:, hc * L + ci * 128: hc * L + (ci + 1) * 128],
                B_bf[:, hc * 512:(hc + 1) * 512],
                start=(hc == 0), stop=(hc == HC - 1)),
               mark="ST" if hc == HC - 1 else None)
        st_bu[ci] = cnt["ST"]
        wt("v", "ST", st_bu[ci])
        if ci >= 1:
            wt("v", "ST", st_cs[ci - 1])
        for m in range(4):
            V_(lambda e, m=m: e.tensor_mul(
                G_k[:, m * 512:(m + 1) * 512], W_km[:, m * 512:(m + 1) * 512],
                ps_bu[:, :]), mark="SV" if m == 3 else None)
        sv_g[ci] = cnt["SV"]
        wt("t", "SV", sv_g[ci])
        if ci >= 1:
            wt("t", "SV", sv_pm[ci - 1])
        for m, ps_dst in ((2, ps_csa), (3, ps_csb)):
            for ct in range(4):
                op("t", lambda eng, m=m, ct=ct, ps_dst=ps_dst: eng.matmul(
                    ps_dst[:, ct * 128:(ct + 1) * 128],
                    G_k[:, m * 512 + ct * 128: m * 512 + (ct + 1) * 128],
                    tril_sb[:, :], start=True, stop=True))
        for mi, m in enumerate((0, 1)):
            for ct in range(4):
                col = mi * 4 + ct
                op("t", lambda eng, m=m, ct=ct, col=col: eng.matmul(
                    ps_cs1[:, col:col + 1],
                    G_k[:, m * 512 + ct * 128: m * 512 + (ct + 1) * 128],
                    tril_sb[:, 127:128], start=True, stop=True),
                   mark="ST" if col == 7 else None)
        st_cs[ci] = cnt["ST"]
        wt("v", "ST", st_cs[ci])
        for ct in range(4):
            pt = ct % 2
            V_(lambda e, pt=pt, ct=ct: e.tensor_mul(
                dscr[:, 0:128], Pm_c[pt][:, 0:128],
                ps_csa[:, ct * 128:(ct + 1) * 128]))
            V_(lambda e, pt=pt, ct=ct: e.tensor_mul(
                dscr2[:, 0:128], Qm_c[pt][:, 0:128],
                ps_csb[:, ct * 128:(ct + 1) * 128]))
            V_(lambda e, ct=ct, ci=ci: e.tensor_add(
                x2l[:, ct * L + ci * 128: ct * L + (ci + 1) * 128],
                dscr[:, 0:128], dscr2[:, 0:128]))
            V_(lambda e, pt=pt, ct=ct: e.tensor_mul(
                r_col[:, 0:1], Pm_c[pt][:, 127:128], ps_cs1[:, ct:ct + 1]))
            V_(lambda e, pt=pt, ct=ct: e.tensor_mul(
                r_col[:, 1:2], Qm_c[pt][:, 127:128], ps_cs1[:, 4 + ct:5 + ct]))
            V_(lambda e, ct=ct, ci=ci: e.tensor_add(
                xl1[:, ct * NCH + ci: ct * NCH + ci + 1],
                r_col[:, 0:1], r_col[:, 1:2]),
               mark="SV" if ct == 3 else None)
        sv_pm[ci] = cnt["SV"]

    for ct in range(4 if lim >= 4 else 0):
        V_(lambda e, ct=ct: e.memset(
            S1_sb[:, ct * (NCH + 1): ct * (NCH + 1) + 1], 0.0))
        V_(lambda e, ct=ct: e.memset(
            S2_sb[:, ct * (NCH + 1): ct * (NCH + 1) + 1], 0.0))
    for ci in range(NCH if lim >= 4 else 0):
        for ct in range(4):
            pt = ct % 2
            c0 = ct * (NCH + 1) + ci
            xl1c = xl1[:, ct * NCH + ci: ct * NCH + ci + 1]
            xl2c = x2l[:, ct * L + ci * 128 + 127: ct * L + ci * 128 + 128]
            V_(lambda e, pt=pt, c0=c0: e.tensor_scalar_mul(
                r_col[:, 0:1], S1_sb[:, c0:c0 + 1], PRM["mc11"][:, pt:pt + 1]))
            V_(lambda e, pt=pt, c0=c0: e.scalar_tensor_tensor(
                r_col[:, 0:1], S2_sb[:, c0:c0 + 1], PRM["mc12"][:, pt:pt + 1],
                r_col[:, 0:1], ALU.mult, ALU.add))
            V_(lambda e, c0=c0, xl1c=xl1c: e.tensor_add(
                S1_sb[:, c0 + 1:c0 + 2], r_col[:, 0:1], xl1c))
            V_(lambda e, pt=pt, c0=c0: e.tensor_scalar_mul(
                r_col[:, 1:2], S1_sb[:, c0:c0 + 1], PRM["mc21"][:, pt:pt + 1]))
            V_(lambda e, pt=pt, c0=c0: e.scalar_tensor_tensor(
                r_col[:, 1:2], S2_sb[:, c0:c0 + 1], PRM["mc22"][:, pt:pt + 1],
                r_col[:, 1:2], ALU.mult, ALU.add))
            V_(lambda e, c0=c0, xl2c=xl2c: e.tensor_add(
                S2_sb[:, c0 + 1:c0 + 2], r_col[:, 1:2], xl2c))
    sv_x2g = [None] * NCH
    for ci in range(NCH if lim >= 4 else 0):
        for ct in range(4):
            pt = ct % 2
            c0 = ct * (NCH + 1) + ci
            tsl = slice(ct * L + ci * 128, ct * L + (ci + 1) * 128)
            V_(lambda e, pt=pt, c0=c0, tsl=tsl: e.scalar_tensor_tensor(
                dscr[:, 0:128], K21_c[pt][:, :], S1_sb[:, c0:c0 + 1],
                x2l[:, tsl], ALU.mult, ALU.add))
            V_(lambda e, pt=pt, c0=c0, tsl=tsl: e.scalar_tensor_tensor(
                x2g[:, tsl], K22_c[pt][:, :], S2_sb[:, c0:c0 + 1],
                dscr[:, 0:128], ALU.mult, ALU.add),
               mark="SV" if ct == 3 else None)
        sv_x2g[ci] = cnt["SV"]

    sv_sev = []
    for tq in range(4 if lim >= 4 else 0):
        wt("t", "SV", sv_x2g[tq * 4 + 3])
        for hc in range(HC):
            par = (tq * HC + hc) % 2
            if sv_sev:
                wt("t", "SV", sv_sev[-1])
            for ct in range(4):
                op("t", lambda eng, tq=tq, hc=hc, ct=ct, par=par: eng.matmul(
                    ps_cy[par][:, :],
                    C_bf[:, ct * H + hc * 128: ct * H + (hc + 1) * 128],
                    x2g[:, ct * L + tq * 512: ct * L + (tq + 1) * 512],
                    start=(ct == 0), stop=(ct == 3)),
                   mark="ST" if ct == 3 else None)
            st_now = cnt["ST"]
            wt("v", "ST", st_now)
            sv_sev.append(V_(lambda e, tq=tq, hc=hc, par=par: e.scalar_tensor_tensor(
                s_sb[:, hc * L + tq * 512: hc * L + (tq + 1) * 512],
                xnb[:, hc * L + tq * 512: hc * L + (tq + 1) * 512],
                dcol[:, hc:hc + 1], ps_cy[par][:, :], ALU.mult, ALU.add),
                mark="SV"))

    if dbg and lim >= 4:
        dump("x2l", x2l)
        dump("x2g", x2g)
        dump("xl1", xl1)
        dump("S1_sb", S1_sb)
        dump("S2_sb", S2_sb)
        dump("G_k", G_k)
        dump("s_sb", s_sb)
    run_block(p0)
    ps1.close()
    ps0.close()
    es1.close()
    es0.close()
    es01.close()

    y_sb = sbF("y_sb", [128, HC * HALF], F32)
    act_sb = sbF("act_sb", [128, IC * HALF], BF16)
    wg_sb = [sbF(f"wg_sb{i}", [128, H], BF16) for i in range(2)]
    wu_sb = [sbF(f"wu_sb{i}", [128, H], BF16) for i in range(2)]
    wd_sb = sbF("wd_sb", [128, IC * H], BF16)
    sil_t = [sbF(f"sil_t{i}", [128, 512], F32) for i in range(2)]
    ftmp = [sbF(f"ftmp{i}", [128, 512], F32) for i in range(2)]
    x_re = [sbF(f"x_re{i}", [128, H], F32) for i in range(2)]
    out_tok = [sbF(f"out_tok{i}", [128, H], F32) for i in range(2)]
    g_t = [sbF(f"g_t{i}", [128, 128], F32) for i in range(2)]
    sg_t = [sbF(f"sg_t{i}", [128, 128], BF16) for i in range(2)]
    cw_row = [sbF(f"cw_row{i}", [1, HALF], F32) for i in range(2)]
    cwbdbg = sbF("cwbdbg", [128, HALF], F32)
    gu_consumed = {}
    wld = {}
    for half in range(2 if lim >= 5 else 0):
        plan = new_plan()
        pF = plan
        psF = ExitStack()
        ps_g = [psF.enter_context(nc.psum_tensor(f"ps_g{half}_{i}", [128, 512], F32))
                for i in range(2)]
        ps_u = [psF.enter_context(nc.psum_tensor(f"ps_u{half}_{i}", [128, 512], F32))
                for i in range(2)]
        ps_y = [psF.enter_context(nc.psum_tensor(f"ps_y{half}_{i}", [128, 512], F32))
                for i in range(2)]
        ps_cwb = psF.enter_context(nc.psum_tensor(f"ps_cwb{half}", [128, 1024], F32))

        act_sv = {}
        dn_st = {}
        ye_sv = {}
        cwbmm_st = {}
        for e_i in range(nexlim):
            shared = e_i >= E

            def chunk_gidx(ic):
                return (half * NEX + e_i) * IC + ic

            def plan_wload(ic, e_i=None, shared=None):
                gidx = chunk_gidx(ic)
                bpar = gidx % 2
                if gidx - 2 in gu_consumed:
                    wt("g", "ST", gu_consumed[gidx - 2])
                for w_i in range(2):
                    buf = (wg_sb if w_i == 0 else wu_sb)[bpar]
                    s_nm = (f"SWg{bpar}" if w_i == 0 else f"SWu{bpar}")
                    if shared:
                        sic = (e_i - E) * IC + ic
                        dram = sgT_d if w_i == 0 else suT_d
                        src_ap = dram.ap()[sic].rearrange(
                            "(hcp p) i -> p hcp i", p=128)
                    else:
                        dram = wgT_d if w_i == 0 else wuT_d
                        src_ap = dram.ap()[e_i, ic].rearrange(
                            "(hcp p) i -> p hcp i", p=128)
                    op("g", lambda eng, buf=buf, src_ap=src_ap: eng.dma_start(
                        out=buf.ap().rearrange("p (hcp i) -> p hcp i", hcp=HC),
                        in_=src_ap), mark=s_nm, inc=16)
                    wld[(gidx, w_i)] = (s_nm, cnt[s_nm])

            plan_wload(0, e_i=e_i, shared=shared)
            plan_wload(1, e_i=e_i, shared=shared)
            for ic in range(IC):
                if (e_i - 1, 15) in dn_st:
                    wt("g", "ST", dn_st[(e_i - 1, 15)])
                if shared:
                    r0 = (e_i - E) * I_E + ic * 128
                    src_ap = sdT_d[r0:r0 + 128, :]
                else:
                    src_ap = wdT_d.ap()[e_i, ic * 128:(ic + 1) * 128, :]
                op("g", lambda eng, ic=ic, src_ap=src_ap: eng.dma_start(
                    out=wd_sb[:, ic * H:(ic + 1) * H], in_=src_ap),
                   mark="SWd", inc=16)
            wdld_swd = cnt["SWd"]

            if not shared:
                rpar = e_i % 2
                rsem = "SCa" if rpar == 0 else "SCb"
                if (e_i - 2) in cwbmm_st:
                    wt("y", "ST", cwbmm_st[e_i - 2])
                op("y", lambda eng, e_i=e_i, rpar=rpar, half=half:
                   eng.dma_start(out=cw_row[rpar][:, :],
                                 in_=cwT[e_i:e_i + 1,
                                         half * HALF:(half + 1) * HALF]),
                   mark=rsem, inc=16)
                sx_row = cnt[rsem]

                def plan_cwb(rpar=rpar, rsem=rsem, sx_row=sx_row, e_i=e_i):
                    if (e_i - 1, 21) in act_sv:
                        wt("t", "SV", act_sv[(e_i - 1, 21)])
                    wt("t", rsem, sx_row)
                    for piece in range(2):
                        op("t", lambda eng, rpar=rpar, piece=piece: eng.matmul(
                            ps_cwb[:, piece * 512:(piece + 1) * 512],
                            ones_sb[:, :],
                            cw_row[rpar][:, piece * 512:(piece + 1) * 512],
                            start=True, stop=True),
                           mark="ST" if piece == 1 else None)
                    cwbmm_st[e_i] = cnt["ST"]
            else:
                plan_cwb = None
                if dbg and e_i == 0 and half == 1:
                    wt("v", "ST", cwbmm_st[e_i])
                    V_(lambda e: e.tensor_copy(cwbdbg[:, :], ps_cwb[:, :]),
                       mark="SV")

            for ic in range(IC):
                gidx = chunk_gidx(ic)
                bpar = gidx % 2
                wt("t", *wld[(gidx, 0)])
                wt("t", *wld[(gidx, 1)])
                for ts in range(2):
                    k = ic * 2 + ts
                    a_key = (e_i, k - 2) if k >= 2 else (e_i - 1, 20 + k)
                    if a_key in act_sv:
                        wt("t", "SV", act_sv[a_key])
                    for w_i in range(2):
                        wbuf = (wg_sb if w_i == 0 else wu_sb)[bpar]
                        ps_dst = (ps_g if w_i == 0 else ps_u)[k % 2]
                        for hc in range(HC):
                            op("t", lambda eng, wbuf=wbuf, ps_dst=ps_dst,
                               hc=hc, ts=ts, half=half: eng.matmul(
                                   ps_dst[:, :],
                                   wbuf[:, hc * 128:(hc + 1) * 128],
                                   xnb[:, hc * L + half * HALF + ts * 512:
                                       hc * L + half * HALF + (ts + 1) * 512],
                                   start=(hc == 0), stop=(hc == HC - 1)),
                               mark="ST" if (w_i == 1 and hc == HC - 1) else None)
                    gu_consumed[gidx] = cnt["ST"]
                    st_k = cnt["ST"]
                    if ic == 0 and ts == 0 and plan_cwb is not None:
                        plan_cwb()
                    if ts == 1 and ic + 2 < IC:
                        plan_wload(ic + 2, e_i=e_i, shared=shared)
                    wt("s", "ST", st_k)
                    if (e_i, k - 2) in act_sv:
                        wt("s", "SV", act_sv[(e_i, k - 2)])
                    elif (e_i - 1, 20 + k) in act_sv:
                        wt("s", "SV", act_sv[(e_i - 1, 20 + k)])
                    ss_k = op("s", lambda eng, k=k: eng.activation(
                        sil_t[k % 2][:, :], ps_g[k % 2][:, :], AF.Silu),
                        mark="SS")
                    wt("v", "SS", ss_k)
                    asl = slice(ic * HALF + ts * 512, ic * HALF + (ts + 1) * 512)
                    if shared:
                        act_sv[(e_i, k)] = V_(
                            lambda e, k=k, asl=asl: e.tensor_mul(
                                act_sb[:, asl], sil_t[k % 2][:, :],
                                ps_u[k % 2][:, :]), mark="SV")
                    else:
                        V_(lambda e, k=k: e.tensor_mul(
                            ftmp[k % 2][:, :], sil_t[k % 2][:, :],
                            ps_u[k % 2][:, :]))
                        act_sv[(e_i, k)] = V_(
                            lambda e, k=k, ts=ts, asl=asl: e.tensor_mul(
                                act_sb[:, asl], ftmp[k % 2][:, :],
                                ps_cwb[:, ts * 512:(ts + 1) * 512]), mark="SV")

            wt("t", "SWd", wdld_swd)
            for ts in range(2):
                wt("t", "SV", act_sv[(e_i, 20 + ts)])
                for hc in range(HC):
                    d = hc * 2 + ts
                    y_key = (e_i, d - 2) if d >= 2 else (e_i - 1, 14 + d)
                    if y_key in ye_sv:
                        wt("t", "SV", ye_sv[y_key])
                    for ic in range(IC):
                        op("t", lambda eng, ic=ic, hc=hc, ts=ts, d=d: eng.matmul(
                            ps_y[d % 2][:, :],
                            wd_sb[:, ic * H + hc * 128: ic * H + (hc + 1) * 128],
                            act_sb[:, ic * HALF + ts * 512:
                                   ic * HALF + (ts + 1) * 512],
                            start=(ic == 0), stop=(ic == IC - 1)),
                           mark="ST" if ic == IC - 1 else None)
                    dn_st[(e_i, d)] = cnt["ST"]
                    wt("v", "ST", dn_st[(e_i, d)])
                    ysl = slice(hc * HALF + ts * 512, hc * HALF + (ts + 1) * 512)
                    if e_i == 0:
                        ye_sv[(e_i, d)] = V_(lambda e, d=d, ysl=ysl: e.tensor_copy(
                            y_sb[:, ysl], ps_y[d % 2][:, :]), mark="SV")
                    else:
                        ye_sv[(e_i, d)] = V_(lambda e, d=d, ysl=ysl: e.tensor_add(
                            y_sb[:, ysl], y_sb[:, ysl], ps_y[d % 2][:, :]),
                            mark="SV")

        run_block(pF)
        psF.close()

        plan = new_plan()
        pT = plan
        psT2 = ExitStack()
        ps_o = [psT2.enter_context(nc.psum_tensor(f"ps_o{half}_{i}", [128, H], BF16))
                for i in range(2)]
        sv_sg = {}
        st_tr = {}
        sv_ot = {}
        so_out = {}
        so_xre = {}
        for tt8 in range(8):
            par = tt8 % 2
            rsem = "SRa" if par == 0 else "SRb"
            osem = "SOa" if par == 0 else "SOb"
            if (tt8 - 2) in sv_ot:
                wt("y", "SV", sv_ot[tt8 - 2])
            t0 = half * HALF + tt8 * 128
            op("y", lambda eng, par=par, t0=t0: eng.dma_start(
                out=x_re[par][:, :], in_=x_d[t0:t0 + 128, :]),
               mark=rsem, inc=16)
            so_xre[tt8] = (rsem, cnt[rsem])
            for hc in range(HC):
                kk = tt8 * HC + hc
                p2 = kk % 2
                if (kk - 2) in sv_sg:
                    wt("s", "SV", sv_sg[kk - 2])
                ysl = slice(hc * HALF + tt8 * 128, hc * HALF + (tt8 + 1) * 128)
                ss_g = op("s", lambda eng, p2=p2, ysl=ysl: eng.activation(
                    g_t[p2][:, :], y_sb[:, ysl], AF.Silu), mark="SS")
                wt("v", "SS", ss_g)
                if (kk - 2) in st_tr:
                    wt("v", "ST", st_tr[kk - 2])
                ssl = slice(hc * L + half * HALF + tt8 * 128,
                            hc * L + half * HALF + (tt8 + 1) * 128)
                sv_sg[kk] = V_(lambda e, p2=p2, ssl=ssl: e.tensor_mul(
                    sg_t[p2][:, :], s_sb[:, ssl], g_t[p2][:, :]), mark="SV")
                wt("t", "SV", sv_sg[kk])
                if (tt8 - 2) in sv_ot and hc == 0:
                    wt("t", "SV", sv_ot[tt8 - 2])
                st_tr[kk] = opt_tr(lambda eng, p2=p2, par=par, hc=hc:
                                   eng.transpose(ps_o[par][:, hc * 128:(hc + 1) * 128],
                                                 sg_t[p2][:, :], identb_sb[:, :]),
                                   mark="ST")
            wt("v", "ST", st_tr[tt8 * HC + HC - 1])
            wt("v", *so_xre[tt8])
            if (tt8 - 2) in so_out:
                wt("v", *so_out[tt8 - 2])
            sv_ot[tt8] = V_(lambda e, par=par: e.tensor_add(
                out_tok[par][:, :], x_re[par][:, :], ps_o[par][:, :]),
                mark="SV")
            wt("y", "SV", sv_ot[tt8])
            op("y", lambda eng, par=par, t0=t0: eng.dma_start(
                out=out_d[t0:t0 + 128, :], in_=out_tok[par][:, :]),
               mark=osem, inc=16)
            so_out[tt8] = (osem, cnt[osem])
        if dbg:
            dump(f"y_sb{half}", y_sb)
            if half == 1:
                dump("act_sb", act_sb)
                dump("cwbdbg", cwbdbg)
                dump("cwrow_dbg", cw_row[0])
        if half == 1:
            wt("y", "SOa", cnt["SOa"])
            wt("y", "SOb", cnt["SOb"])
        run_block(pT)
        psT2.close()

    if lim < 5:
        plan = new_plan()
        wt("y", "SOa", cnt["SOa"])
        run_block(plan)
    esF.close()
    es.close()
    return nc


_BUILT = None


def _get_nc():
    global _BUILT
    if _BUILT is None:
        _BUILT = build_nc()
    return _BUILT


def kernel(L0_attn_input_sequence, L0_attn_steps_raw, L0_attn_G_raw,
           L0_attn_A_raw, L0_attn_B_mat, L0_attn_C_mat, L0_attn_D_vec,
           L0_ffn_gate_w, L0_ffn_gate_bias, L0_ffn_gate_proj, L0_ffn_up_proj,
           L0_ffn_down_proj, L0_ffn_shared_gate_w, L0_ffn_shared_up_w,
           L0_ffn_shared_down_w):
    f32 = np.float32
    x = np.ascontiguousarray(np.asarray(L0_attn_input_sequence, f32))
    B_mat = np.asarray(L0_attn_B_mat, f32)
    C_mat = np.asarray(L0_attn_C_mat, f32)
    bT = np.ascontiguousarray(B_mat.transpose(1, 2, 0).reshape(H, 2 * P))
    cT = np.ascontiguousarray(C_mat.transpose(2, 1, 0).reshape(2 * P, H))
    gw = np.ascontiguousarray(np.asarray(L0_ffn_gate_w, f32))
    gbias = np.ascontiguousarray(np.asarray(L0_ffn_gate_bias, f32).reshape(1, 16))
    wgT = np.ascontiguousarray(
        np.asarray(L0_ffn_gate_proj, f32).transpose(0, 2, 1)
        .reshape(E, H, IC, 128).transpose(0, 2, 1, 3))
    wuT = np.ascontiguousarray(
        np.asarray(L0_ffn_up_proj, f32).transpose(0, 2, 1)
        .reshape(E, H, IC, 128).transpose(0, 2, 1, 3))
    wdT = np.ascontiguousarray(np.asarray(L0_ffn_down_proj, f32).transpose(0, 2, 1))
    sgT = np.ascontiguousarray(
        np.asarray(L0_ffn_shared_gate_w, f32).T
        .reshape(H, 2 * IC, 128).transpose(1, 0, 2))
    suT = np.ascontiguousarray(
        np.asarray(L0_ffn_shared_up_w, f32).T
        .reshape(H, 2 * IC, 128).transpose(1, 0, 2))
    sdT = np.ascontiguousarray(np.asarray(L0_ffn_shared_down_w, f32).T)
    tril = np.triu(np.ones((128, 128), f32))
    identf = np.eye(128, dtype=f32)
    identb = np.eye(128, dtype=f32).astype(ml_dtypes.bfloat16)
    onesr = np.ones((1, 128), f32)

    common = dict(
        bT=bT, cT=cT, gw=gw, gbias=gbias,
        dvec=np.ascontiguousarray(np.asarray(L0_attn_D_vec, f32)),
        steps_raw=np.ascontiguousarray(np.asarray(L0_attn_steps_raw, f32)),
        g_raw=np.ascontiguousarray(np.asarray(L0_attn_G_raw, f32)),
        a_raw=np.ascontiguousarray(np.asarray(L0_attn_A_raw, f32)),
        wgT=wgT, wuT=wuT, wdT=wdT, sgT=sgT, suT=suT, sdT=sdT,
        tril=tril, identf=identf, identb=identb, onesr=onesr,
    )
    in_maps = [dict(common, x=np.ascontiguousarray(x[b])) for b in range(8)]
    nc = _get_nc()
    res = run_bass_kernel_spmd(nc, in_maps, core_ids=list(range(8)))
    out = np.stack([np.asarray(r["out"], f32) for r in res.results], axis=0)
    return out


# version 06
# speedup vs baseline: 1.1192x; 1.1192x over previous
import numpy as np
import ml_dtypes
from contextlib import ExitStack

import concourse.bass as bass
import concourse.mybir as mybir
from concourse.bass_utils import run_bass_kernel_spmd

F32 = mybir.dt.float32
BF16 = mybir.dt.bfloat16
AF = mybir.ActivationFunctionType
ALU = mybir.AluOpType
AX = mybir.AxisListType

H = 1024
L = 2048
P = 256
C = 128
NCH = L // C
HC = 8
NTT = L // 128
E = 16
I_E = 1408
IC = 11
NEX = 18
HALF = 1024


def build_nc(dbg=False, lim=99, nexlim=NEX):
    nc = bass.Bass("TRN2", target_bir_lowering=False)
    dbg_drams = {}

    x_d = nc.dram_tensor("x", [L, H], F32, kind="ExternalInput")
    out_d = nc.dram_tensor("out", [L, H], F32, kind="ExternalOutput")
    bT_d = nc.dram_tensor("bT", [H, 2 * P], F32, kind="ExternalInput")
    cT_d = nc.dram_tensor("cT", [2 * P, H], F32, kind="ExternalInput")
    gw_d = nc.dram_tensor("gw", [16, H], F32, kind="ExternalInput")
    gbias_d = nc.dram_tensor("gbias", [1, 16], F32, kind="ExternalInput")
    dvec_d = nc.dram_tensor("dvec", [H], F32, kind="ExternalInput")
    steps_d = nc.dram_tensor("steps_raw", [P], F32, kind="ExternalInput")
    graw_d = nc.dram_tensor("g_raw", [P], F32, kind="ExternalInput")
    araw_d = nc.dram_tensor("a_raw", [P], F32, kind="ExternalInput")
    wgT_d = nc.dram_tensor("wgT", [E, IC, H, 128], F32, kind="ExternalInput")
    wuT_d = nc.dram_tensor("wuT", [E, IC, H, 128], F32, kind="ExternalInput")
    wdT_d = nc.dram_tensor("wdT", [E, I_E, H], F32, kind="ExternalInput")
    sgT_d = nc.dram_tensor("sgT", [2 * IC, H, 128], F32, kind="ExternalInput")
    suT_d = nc.dram_tensor("suT", [2 * IC, H, 128], F32, kind="ExternalInput")
    sdT_d = nc.dram_tensor("sdT", [2 * I_E, H], F32, kind="ExternalInput")
    tril_d = nc.dram_tensor("tril", [128, 128], F32, kind="ExternalInput")
    identf_d = nc.dram_tensor("identf", [128, 128], F32, kind="ExternalInput")
    identb_d = nc.dram_tensor("identb", [128, 128], BF16, kind="ExternalInput")
    ones_d = nc.dram_tensor("onesr", [1, 128], F32, kind="ExternalInput")

    es = ExitStack()
    es01 = ExitStack()
    es0 = ExitStack()
    es1 = ExitStack()
    esF = ExitStack()

    def mk_sb(stack):
        return lambda name, shape, dt: stack.enter_context(
            nc.sbuf_tensor(name, shape, dt))

    sb = mk_sb(es)
    sb01 = mk_sb(es01)
    sb0 = mk_sb(es0)
    sb1 = mk_sb(es1)
    sbF = mk_sb(esF)
    sem = lambda name: es.enter_context(nc.semaphore(name))

    xnb = sb("xnb", [128, HC * L], BF16)
    s_sb = sb("s_sb", [128, HC * L], BF16)
    identf_sb = sb("identf_sb", [128, 128], F32)
    identb_sb = sb("identb_sb", [128, 128], BF16)
    ones_sb = sb("ones_sb", [1, 128], F32)
    gwT_sb = sb("gwT_sb", [128, 128], F32)
    gbias_sb = sb("gbias_sb", [1, 16], F32)
    bias_bc = sb("bias_bc", [128, 16], F32)
    dcol = sb("dcol", [128, HC], F32)
    prm = sb("prm", [128, 2], F32)
    graw_sb = sb("graw_sb", [128, 2], F32)
    araw_sb = sb("araw_sb", [128, 2], F32)
    PRM = {}
    for nm in ("steps", "Sv", "rS", "alpha", "m12", "m21", "m22", "ap2", "det",
               "beta", "rbeta", "n11", "n12", "n21", "n22", "sig2", "bp", "bq",
               "mc11", "mc12", "mc21", "mc22"):
        PRM[nm] = sb("prm_" + nm, [128, 2], F32)
    scr1 = sb("scr1", [128, 2], F32)
    scr2 = sb("scr2", [128, 2], F32)
    eps_col = sb("eps_col", [128, 1], F32)
    mpq = sb("mpq", [128, 4], F32)
    dscr = sb("dscr", [128, 130], F32)
    dscr2 = sb("dscr2", [128, 130], F32)
    r_col = sb("r_col", [128, 6], F32)
    cwT = sb("cwT", [16, L], F32)

    tril_sb = sb01("tril_sb", [128, 128], F32)
    Pm_c = [sb01(f"Pm_c{p_}", [128, C + 1], F32) for p_ in range(2)]
    Qm_c = [sb01(f"Qm_c{p_}", [128, C + 1], F32) for p_ in range(2)]
    K21_c = [sb01(f"K21_c{p_}", [128, C], F32) for p_ in range(2)]
    K22_c = [sb01(f"K22_c{p_}", [128, C], F32) for p_ in range(2)]
    W_km = sb01("W_km", [128, 4 * 512], F32)
    B_bf = sb01("B_bf", [128, HC * 512], BF16)
    C_bf = sb01("C_bf", [128, 4 * H], BF16)

    Pt_c = [sb0(f"Pt_c{p_}", [128, C], F32) for p_ in range(2)]
    Qt_c = [sb0(f"Qt_c{p_}", [128, C], F32) for p_ in range(2)]
    wcm = sb0("wcm", [128, 128], F32)
    x_tok = [sb0(f"x_tok{i}", [128, H], F32) for i in range(2)]
    xn_tok = sb0("xn_tok", [128, H], F32)
    sq_tok = sb0("sq_tok", [128, H], F32)
    xn_f = sb0("xn_f", [128, H], F32)
    stat = sb0("stat", [128, 8], F32)
    sd_t = sb0("sd_t", [128, 1], F32)
    logits = sb0("logits", [128, NTT * 16], F32)
    scores = sb0("scores", [128, NTT * 16], F32)
    r_sfc = sb0("r_sfc", [128, NTT * 16], F32)
    r_eq = sb0("r_eq", [128, NTT * 16], F32)
    r_tmp = sb0("r_tmp", [128, NTT * 16], F32)
    r_w = sb0("r_w", [128, NTT * 16], F32)
    r_m64a = sb0("r_m64a", [128, 64], F32)
    r_m64b = sb0("r_m64b", [128, 64], F32)
    r_gs64 = sb0("r_gs64", [128, 64], F32)
    r_g16a = sb0("r_g16a", [128, 16], F32)
    r_g16b = sb0("r_g16b", [128, 16], F32)
    cw_sb = sb0("cw_sb", [128, NTT * 16], F32)
    st_pg = sb0("st_pg", [8, 128], F32)
    st_dv = sb0("st_dv", [8, 128], F32)
    st_gw = sb0("st_gw", [16, H], F32)
    statall = sb0("statall", [128, NTT * 8], F32)
    xdbg = sb0("xdbg", [128, 2 * H], F32)

    names = ["SXs", "SXa", "SXb", "SWbc", "SWg0", "SWg1", "SWu0", "SWu1",
             "SWd", "SCa", "SCb", "SRa", "SRb", "SOa", "SOb",
             "ST", "SV", "SS", "SVD", "SSD"]
    SEMS = {n: sem(n) for n in names}
    cnt = {n: 0 for n in names}

    ENG = ("t", "v", "s", "y", "g")
    plan = None

    def new_plan():
        return {e: [] for e in ENG}

    def run_block(p):
        with nc.Block() as block:
            @block.tensor
            def _(eng):
                for f in p["t"]:
                    f(eng)

            @block.vector
            def _(eng):
                for f in p["v"]:
                    f(eng)

            @block.scalar
            def _(eng):
                for f in p["s"]:
                    f(eng)

            @block.sync
            def _(eng):
                for f in p["y"]:
                    f(eng)

            @block.gpsimd
            def _(eng):
                for f in p["g"]:
                    f(eng)

    def wt(e, semname, val):
        if val is None or val <= 0:
            return
        s_, v_ = SEMS[semname], val
        plan[e].append(lambda eng: eng.wait_ge(s_, v_))

    def bump(semname, k):
        cnt[semname] += k
        return cnt[semname]

    def op(e, make, mark=None, inc=1):
        if e in ("v", "s"):
            if mark is None:
                mark_, inc_ = ("SVD" if e == "v" else "SSD"), 1
            else:
                mark_, inc_ = mark, inc
            s_ = SEMS[mark_]
            plan[e].append(lambda eng: make(eng).then_inc(s_, inc_))
            n_ = bump(mark_, inc_)
            plan[e].append(lambda eng: eng.wait_ge(s_, n_))
            return n_ if mark is not None else None
        if mark is None:
            plan[e].append(lambda eng: make(eng))
            return None
        s_ = SEMS[mark]
        k_ = inc
        plan[e].append(lambda eng: make(eng).then_inc(s_, k_))
        return bump(mark, inc)

    def V_(make, mark=None):
        return op("v", make, mark=mark)

    def dump(name, handle):
        if not dbg:
            return
        shp = list(handle.shape)
        d = nc.dram_tensor("dbg_" + name, shp, handle.dtype,
                           kind="ExternalOutput")
        dbg_drams[name] = d
        for s_nm in ("SV", "SS", "ST"):
            wt("y", s_nm, cnt[s_nm])
        op("y", lambda eng, d=d, h=handle: eng.dma_start(
            out=d[:, :] if len(shp) == 2 else d[:], in_=h[:, :]
            if len(shp) == 2 else h[:]), mark="SOa", inc=16)

    def opt_tr(make, mark=None):
        r = op("t", make, mark=mark)
        plan["t"].append(lambda eng: eng.drain())
        return r

    plan = new_plan()
    p0 = plan
    ps0 = ExitStack()
    psT = ps0.enter_context(nc.psum_tensor("psT", [128, 512], F32))
    ps_gate = ps0.enter_context(nc.psum_tensor("ps_gate", [128, 16], F32))
    ps_cwt = ps0.enter_context(nc.psum_tensor("ps_cwt", [16, 128], F32))
    ps_head = ExitStack()
    ps_w = ps_head.enter_context(nc.psum_tensor("ps_w", [128, 128], F32))
    ps_tr0 = ps_head.enter_context(nc.psum_tensor("ps_tr0", [128, 272], F32))
    ps_bc = ps_tr0

    for dst, src in [
        (tril_sb[:, :], tril_d[:, :]),
        (identf_sb[:, :], identf_d[:, :]),
        (identb_sb[:, :], identb_d[:, :]),
        (ones_sb[:, :], ones_d[:, :]),
        (gbias_sb[:, :], gbias_d[:, :]),
        (st_pg[0:2, :], steps_d.ap().rearrange("(pt p) -> pt p", p=128)),
        (st_pg[2:4, :], graw_d.ap().rearrange("(pt p) -> pt p", p=128)),
        (st_pg[4:6, :], araw_d.ap().rearrange("(pt p) -> pt p", p=128)),
        (st_dv[:, :], dvec_d.ap().rearrange("(hc p) -> hc p", p=128)),
        (st_gw[:, :], gw_d[:, :]),
    ]:
        op("y", lambda eng, d=dst, s_=src: eng.dma_start(out=d, in_=s_),
           mark="SXs", inc=16)
    SX_setup = cnt["SXs"]
    wt("t", "SXs", SX_setup)
    opt_tr(lambda eng: eng.transpose(ps_tr0[:, 0:8], st_pg[:, :],
                                     identf_sb[0:8, 0:8]))
    opt_tr(lambda eng: eng.transpose(ps_tr0[:, 8:16], st_dv[:, :],
                                     identf_sb[0:8, 0:8]))
    for hc in range(HC):
        opt_tr(lambda eng, hc=hc: eng.transpose(
            ps_tr0[:, 16 + hc * 16:32 + hc * 16],
            st_gw[0:16, hc * 128:(hc + 1) * 128], identf_sb[0:16, 0:16]),
            mark="ST" if hc == HC - 1 else None)
    ST_par = cnt["ST"]
    wt("v", "ST", ST_par)
    V_(lambda e: e.tensor_copy(prm[:, :], ps_tr0[:, 0:2]))
    V_(lambda e: e.tensor_copy(graw_sb[:, :], ps_tr0[:, 2:4]))
    V_(lambda e: e.tensor_copy(araw_sb[:, :], ps_tr0[:, 4:6]))
    V_(lambda e: e.tensor_copy(dcol[:, :], ps_tr0[:, 8:16]))
    SV_par = V_(lambda e: e.tensor_copy(gwT_sb[:, :], ps_tr0[:, 16:144]),
                mark="SV")

    x_sx = [None] * NTT
    xre_guard = [None] * NTT
    sv_xn = [None] * NTT
    st_tp = [None] * NTT
    sv_evac = [None] * NTT
    st_gate = [None] * NTT
    ss_sq = [None] * NTT

    def plan_x_load(tt):
        if tt >= 2:
            sv_, ss_g_ = xre_guard[tt - 2]
            wt("y", "SV", sv_)
            wt("y", "SS", ss_g_)
        s_nm = "SXa" if tt % 2 == 0 else "SXb"
        op("y", lambda eng, tt=tt: eng.dma_start(
            out=x_tok[tt % 2][:, :], in_=x_d[tt * 128:(tt + 1) * 128, :]),
           mark=s_nm, inc=16)
        x_sx[tt] = (s_nm, cnt[s_nm])

    for hc in range(HC if lim >= 4 else 0):
        op("g", lambda eng, hc=hc: eng.dma_start(
            out=B_bf[:, hc * 512:(hc + 1) * 512],
            in_=bT_d[hc * 128:(hc + 1) * 128, :]), mark="SWbc", inc=16)
    for ct in range(4 if lim >= 4 else 0):
        op("g", lambda eng, ct=ct: eng.dma_start(
            out=C_bf[:, ct * H:(ct + 1) * H],
            in_=cT_d[ct * 128:(ct + 1) * 128, :]), mark="SWbc", inc=16)
    SW_bc = cnt["SWbc"]
    wt("v", "SWbc", SW_bc)
    for ct in ((2, 3) if lim >= 4 else ()):
        V_(lambda e, ct=ct: e.tensor_scalar_mul(
            C_bf[:, ct * H:(ct + 1) * H], C_bf[:, ct * H:(ct + 1) * H], -1.0),
           mark="SV" if ct == 3 else None)

    V_(lambda e: e.memset(eps_col[:, :], 1e-5))
    wt("s", "SV", SV_par)
    op("s", lambda eng: eng.activation(PRM["steps"][:, :], prm[:, :], AF.Sigmoid),
       mark="SS")
    SS_steps = cnt["SS"]

    wt("v", "SS", SS_steps)
    st_ = PRM["steps"]
    V_(lambda e: e.tensor_scalar_max(graw_sb[:, :], graw_sb[:, :], 0.0))
    V_(lambda e: e.tensor_scalar_max(araw_sb[:, :], araw_sb[:, :], 0.0))
    V_(lambda e: e.tensor_mul(scr1[:, :], st_[:, :], graw_sb[:, :]))
    V_(lambda e: e.tensor_scalar_add(PRM["Sv"][:, :], scr1[:, :], 1.0))
    V_(lambda e: e.reciprocal(PRM["rS"][:, :], PRM["Sv"][:, :]))
    V_(lambda e: e.tensor_mul(scr1[:, :], st_[:, :], st_[:, :]))
    V_(lambda e: e.tensor_mul(scr1[:, :], scr1[:, :], araw_sb[:, :]))
    sv_a0 = V_(lambda e: e.tensor_mul(scr1[:, :], scr1[:, :], PRM["rS"][:, :]),
               mark="SV")
    wt("s", "SV", sv_a0)
    ss_tanh = op("s", lambda eng: eng.activation(
        scr2[:, :], scr1[:, :], AF.Tanh, scale=1.0 / 1.99), mark="SS")
    wt("v", "SS", ss_tanh)
    V_(lambda e: e.tensor_scalar_mul(PRM["alpha"][:, :], scr2[:, :], 1.99))
    V_(lambda e: e.reciprocal(scr1[:, :], st_[:, :]))
    V_(lambda e: e.tensor_mul(scr2[:, :], PRM["alpha"][:, :], scr1[:, :]))
    V_(lambda e: e.tensor_scalar_mul(PRM["m12"][:, :], scr2[:, :], -1.0))
    V_(lambda e: e.tensor_mul(PRM["m21"][:, :], st_[:, :], PRM["rS"][:, :]))
    V_(lambda e: e.tensor_scalar(PRM["m22"][:, :], PRM["alpha"][:, :], -1.0, 1.0,
                                 ALU.mult, ALU.add))
    V_(lambda e: e.tensor_add(scr1[:, :], PRM["rS"][:, :], PRM["m22"][:, :]))
    V_(lambda e: e.tensor_scalar_mul(PRM["ap2"][:, :], scr1[:, :], 0.5))
    V_(lambda e: e.tensor_mul(scr1[:, :], PRM["rS"][:, :], PRM["m22"][:, :]))
    V_(lambda e: e.tensor_mul(scr2[:, :], PRM["m12"][:, :], PRM["m21"][:, :]))
    V_(lambda e: e.tensor_sub(PRM["det"][:, :], scr1[:, :], scr2[:, :]))
    V_(lambda e: e.tensor_mul(scr1[:, :], PRM["ap2"][:, :], PRM["ap2"][:, :]))
    V_(lambda e: e.tensor_sub(scr2[:, :], PRM["det"][:, :], scr1[:, :]))
    sv_disc = V_(lambda e: e.tensor_scalar_max(scr2[:, :], scr2[:, :], 1e-20),
                 mark="SV")
    wt("s", "SV", sv_disc)
    ss_beta = op("s", lambda eng: eng.activation(
        PRM["beta"][:, :], scr2[:, :], AF.Sqrt), mark="SS")
    wt("v", "SS", ss_beta)
    V_(lambda e: e.reciprocal(PRM["rbeta"][:, :], PRM["beta"][:, :]))
    V_(lambda e: e.tensor_sub(scr1[:, :], PRM["rS"][:, :], PRM["ap2"][:, :]))
    V_(lambda e: e.tensor_mul(PRM["n11"][:, :], scr1[:, :], PRM["rbeta"][:, :]))
    V_(lambda e: e.tensor_mul(PRM["n12"][:, :], PRM["m12"][:, :], PRM["rbeta"][:, :]))
    V_(lambda e: e.tensor_mul(PRM["n21"][:, :], PRM["m21"][:, :], PRM["rbeta"][:, :]))
    V_(lambda e: e.tensor_sub(scr1[:, :], PRM["m22"][:, :], PRM["ap2"][:, :]))
    V_(lambda e: e.tensor_mul(PRM["n22"][:, :], scr1[:, :], PRM["rbeta"][:, :]))
    V_(lambda e: e.tensor_mul(PRM["sig2"][:, :], st_[:, :], PRM["m21"][:, :]))
    V_(lambda e: e.reciprocal(scr1[:, :], PRM["det"][:, :]))
    V_(lambda e: e.tensor_mul(PRM["bp"][:, :], PRM["ap2"][:, :], scr1[:, :]))
    V_(lambda e: e.tensor_mul(scr2[:, :], PRM["beta"][:, :], scr1[:, :]))
    V_(lambda e: e.tensor_scalar_mul(PRM["bq"][:, :], scr2[:, :], -1.0))

    def build_tables(Pt_arr, Qt_arr, re_nm, im_nm, ncols):
        for pt in range(2):
            Pm, Qm = Pt_arr[pt], Qt_arr[pt]
            zre = PRM[re_nm][:, pt:pt + 1]
            zim = PRM[im_nm][:, pt:pt + 1]
            V_(lambda e, Pm=Pm: e.memset(Pm[:, 0:1], 1.0))
            V_(lambda e, Qm=Qm: e.memset(Qm[:, 0:1], 0.0))
            V_(lambda e, Pm=Pm, zre=zre: e.tensor_copy(Pm[:, 1:2], zre))
            V_(lambda e, Qm=Qm, zim=zim: e.tensor_copy(Qm[:, 1:2], zim))
            c = 2
            while c < ncols:
                V_(lambda e, Pm=Pm, zre=zre, c=c: e.tensor_scalar_mul(
                    mpq[:, 2:3], Pm[:, c - 1:c], zre))
                V_(lambda e, Qm=Qm, zim=zim, c=c: e.tensor_scalar_mul(
                    mpq[:, 3:4], Qm[:, c - 1:c], zim))
                V_(lambda e: e.tensor_sub(mpq[:, 0:1], mpq[:, 2:3], mpq[:, 3:4]))
                V_(lambda e, Pm=Pm, zim=zim, c=c: e.tensor_scalar_mul(
                    mpq[:, 2:3], Pm[:, c - 1:c], zim))
                V_(lambda e, Qm=Qm, zre=zre, c=c: e.tensor_scalar_mul(
                    mpq[:, 3:4], Qm[:, c - 1:c], zre))
                V_(lambda e: e.tensor_add(mpq[:, 1:2], mpq[:, 2:3], mpq[:, 3:4]))
                n = min(c, ncols - c)
                V_(lambda e, Pm=Pm, n=n: e.tensor_scalar_mul(
                    dscr[:, 0:n], Pm[:, 0:n], mpq[:, 0:1]))
                V_(lambda e, Qm=Qm, n=n: e.tensor_scalar_mul(
                    dscr2[:, 0:n], Qm[:, 0:n], mpq[:, 1:2]))
                V_(lambda e, Pm=Pm, c=c, n=n: e.tensor_sub(
                    Pm[:, c:c + n], dscr[:, 0:n], dscr2[:, 0:n]))
                V_(lambda e, Pm=Pm, n=n: e.tensor_scalar_mul(
                    dscr[:, 0:n], Pm[:, 0:n], mpq[:, 1:2]))
                V_(lambda e, Qm=Qm, n=n: e.tensor_scalar_mul(
                    dscr2[:, 0:n], Qm[:, 0:n], mpq[:, 0:1]))
                V_(lambda e, Qm=Qm, c=c, n=n: e.tensor_add(
                    Qm[:, c:c + n], dscr[:, 0:n], dscr2[:, 0:n]))
                c *= 2

    if lim >= 1:
        build_tables(Pm_c, Qm_c, "ap2", "beta", C + 1)
        build_tables(Pt_c, Qt_c, "bp", "bq", C)

    for pt in range(2 if lim >= 1 else 0):
        V_(lambda e, pt=pt: e.tensor_scalar_mul(
            K21_c[pt][:, :], Qm_c[pt][:, 1:C + 1], PRM["n21"][:, pt:pt + 1]))
        V_(lambda e, pt=pt: e.tensor_scalar_mul(
            dscr[:, 0:C], Qm_c[pt][:, 1:C + 1], PRM["n22"][:, pt:pt + 1]))
        V_(lambda e, pt=pt: e.tensor_add(
            K22_c[pt][:, :], Pm_c[pt][:, 1:C + 1], dscr[:, 0:C]))
        V_(lambda e, pt=pt: e.tensor_scalar_mul(
            scr1[:, pt:pt + 1], Qm_c[pt][:, C:C + 1], PRM["n11"][:, pt:pt + 1]))
        V_(lambda e, pt=pt: e.tensor_add(
            PRM["mc11"][:, pt:pt + 1], Pm_c[pt][:, C:C + 1], scr1[:, pt:pt + 1]))
        V_(lambda e, pt=pt: e.tensor_scalar_mul(
            PRM["mc12"][:, pt:pt + 1], Qm_c[pt][:, C:C + 1], PRM["n12"][:, pt:pt + 1]))
        V_(lambda e, pt=pt: e.tensor_copy(
            PRM["mc21"][:, pt:pt + 1], K21_c[pt][:, C - 1:C]))
        V_(lambda e, pt=pt: e.tensor_copy(
            PRM["mc22"][:, pt:pt + 1], K22_c[pt][:, C - 1:C]))

    prev_tr = None
    for mi in range(4 if lim >= 1 else 0):
        for pt in range(2):
            Ptt, Qtt = Pt_c[pt], Qt_c[pt]
            s1c = PRM["m21"][:, pt:pt + 1]
            s2c = PRM["sig2"][:, pt:pt + 1]
            n1c = PRM["n11" if mi < 2 else "n21"][:, pt:pt + 1]
            n2c = PRM["n12" if mi < 2 else "n22"][:, pt:pt + 1]
            flip = (mi % 2 == 1)
            if mi < 2:
                base, other = (Qtt, Ptt) if not flip else (Ptt, Qtt)
                V_(lambda e, base=base, n1c=n1c: e.tensor_scalar_mul(
                    dscr[:, 0:C], base[:, :], n1c))
                if not flip:
                    V_(lambda e, other=other: e.tensor_add(
                        dscr[:, 0:C], other[:, :], dscr[:, 0:C]))
                else:
                    V_(lambda e, other=other: e.tensor_sub(
                        dscr[:, 0:C], dscr[:, 0:C], other[:, :]))
                V_(lambda e, s1c=s1c: e.tensor_scalar_mul(
                    dscr[:, 0:C], dscr[:, 0:C], s1c))
                src2 = Qtt if not flip else Ptt
                V_(lambda e, src2=src2, n2c=n2c: e.tensor_scalar_mul(
                    dscr2[:, 0:C], src2[:, :], n2c))
                V_(lambda e, s2c=s2c: e.tensor_scalar_mul(
                    dscr2[:, 0:C], dscr2[:, 0:C], s2c))
            else:
                src1 = Qtt if not flip else Ptt
                V_(lambda e, src1=src1, n1c=n1c: e.tensor_scalar_mul(
                    dscr[:, 0:C], src1[:, :], n1c))
                V_(lambda e, s1c=s1c: e.tensor_scalar_mul(
                    dscr[:, 0:C], dscr[:, 0:C], s1c))
                base, other = (Qtt, Ptt) if not flip else (Ptt, Qtt)
                V_(lambda e, base=base, n2c=n2c: e.tensor_scalar_mul(
                    dscr2[:, 0:C], base[:, :], n2c))
                if not flip:
                    V_(lambda e, other=other: e.tensor_add(
                        dscr2[:, 0:C], other[:, :], dscr2[:, 0:C]))
                else:
                    V_(lambda e, other=other: e.tensor_sub(
                        dscr2[:, 0:C], dscr2[:, 0:C], other[:, :]))
                V_(lambda e, s2c=s2c: e.tensor_scalar_mul(
                    dscr2[:, 0:C], dscr2[:, 0:C], s2c))
            if prev_tr is not None:
                wt("v", "ST", prev_tr)
            sv_w = V_(lambda e: e.tensor_add(wcm[:, :], dscr[:, 0:C],
                                             dscr2[:, 0:C]), mark="SV")
            wt("t", "SV", sv_w)
            prev_tr = opt_tr(lambda eng: eng.transpose(
                ps_w[:, :], wcm[:, :], identf_sb[:, :]), mark="ST")
            wt("v", "ST", prev_tr)
            for j in range(2):
                col = mi * 512 + j * 256 + pt * 128
                V_(lambda e, col=col: e.tensor_copy(
                    W_km[:, col:col + 128], ps_w[:, :]),
                   mark="SV" if j == 1 else None)

    if lim < 1:
        if dbg:
            for nm_ in PRM:
                dump("prm_" + nm_, PRM[nm_])
            for pt_ in range(2):
                dump(f"Pm_c{pt_}", Pm_c[pt_])
                dump(f"Qm_c{pt_}", Qm_c[pt_])
        run_block(p0)
        ps0.close()
        es0.close()
        es01.close()
        plan = new_plan()
        wt("y", "SOa", cnt["SOa"])
        run_block(plan)
        esF.close()
        es.close()
        return nc
    wt("t", "SXs", SX_setup)
    st_bias = op("t", lambda eng: eng.matmul(
        ps_tr0[:, 256:272], ones_sb[:, :], gbias_sb[:, :], start=True,
        stop=True),
        mark="ST")
    wt("v", "ST", st_bias)
    V_(lambda e: e.tensor_copy(bias_bc[:, :], ps_tr0[:, 256:272]))

    if lim >= 2:
        plan_x_load(0)
        plan_x_load(1)
    for tt in range(NTT if lim >= 3 else 0):
        par = tt % 2
        wt("s", x_sx[tt][0], x_sx[tt][1])
        if tt >= 1:
            wt("s", "SV", sv_xn[tt - 1])
        op("s", lambda eng, par=par: eng.activation(
            sq_tok[:, :], x_tok[par][:, :], AF.Square,
            accum_out=stat[:, 1:2]), mark="SS")
        ss_sq[tt] = op("s", lambda eng, par=par: eng.activation(
            sq_tok[:, :], x_tok[par][:, :], AF.Copy,
            accum_out=stat[:, 0:1]), mark="SS")
        wt("v", x_sx[tt][0], x_sx[tt][1])
        wt("v", "SS", ss_sq[tt])
        if dbg and tt < 2:
            V_(lambda e, par=par, tt=tt: e.tensor_copy(
                xdbg[:, tt * H:(tt + 1) * H], x_tok[par][:, :]))
        if tt >= 1:
            wt("v", "ST", st_tp[tt - 1])
        V_(lambda e: e.tensor_scalar_mul(stat[:, 0:1], stat[:, 0:1], 1.0 / H))
        V_(lambda e: e.tensor_mul(stat[:, 2:3], stat[:, 0:1], stat[:, 0:1]))
        sv_var = V_(lambda e: e.scalar_tensor_tensor(
            stat[:, 3:4], stat[:, 1:2], 1.0 / H, stat[:, 2:3],
            ALU.mult, ALU.subtract), mark="SV")
        wt("s", "SV", sv_var)
        ss_sd = op("s", lambda eng: eng.activation(sd_t[:, :], stat[:, 3:4],
                                                   AF.Sqrt,
                                                   bias=eps_col[:, :]),
                   mark="SS")
        wt("v", "SS", ss_sd)
        V_(lambda e: e.reciprocal(stat[:, 5:6], sd_t[:, :]))
        sv_xn[tt] = V_(lambda e, par=par: e.tensor_scalar(
            xn_tok[:, :], x_tok[par][:, :], stat[:, 0:1], stat[:, 5:6],
            ALU.subtract, ALU.mult), mark="SV")
        if dbg:
            V_(lambda e, tt=tt: e.tensor_copy(
                statall[:, tt * 8:(tt + 1) * 8], stat[:, :]))
        wt("t", "SV", sv_xn[tt])
        for rnd in range(2):
            if tt >= 1 or rnd == 1:
                wt("t", "SV", sv_evac[tt] if rnd == 1 else sv_evac[tt - 1])
            for hq in range(4):
                opt_tr(lambda eng, rnd=rnd, hq=hq: eng.transpose(
                    psT[:, hq * 128:(hq + 1) * 128],
                    xn_tok[:, (rnd * 4 + hq) * 128:(rnd * 4 + hq + 1) * 128],
                    identf_sb[:, :]),
                    mark="ST" if hq == 3 else None)
            st_tp[tt] = cnt["ST"]
            wt("v", "ST", st_tp[tt])
            V_(lambda e, rnd=rnd: e.tensor_copy(
                xn_f[:, rnd * 512:(rnd + 1) * 512], psT[:, :]))
            for hq in range(4):
                hc = rnd * 4 + hq
                mk = "SV" if hq == 3 else None
                sv_evac[tt] = V_(lambda e, tt=tt, hc=hc: e.tensor_copy(
                    xnb[:, hc * L + tt * 128: hc * L + (tt + 1) * 128],
                    xn_f[:, hc * 128:(hc + 1) * 128]), mark=mk)
        wt("t", "SV", sv_evac[tt])
        for hc in range(HC):
            op("t", lambda eng, hc=hc: eng.matmul(
                ps_gate[:, :], xn_f[:, hc * 128:(hc + 1) * 128],
                gwT_sb[:, hc * 16:(hc + 1) * 16],
                start=(hc == 0), stop=(hc == HC - 1)),
               mark="ST" if hc == HC - 1 else None)
        st_gate[tt] = cnt["ST"]
        wt("v", "ST", st_gate[tt])
        sv_logits_last = V_(lambda e, tt=tt: e.tensor_copy(
            logits[:, tt * 16:(tt + 1) * 16], ps_gate[:, :]), mark="SV")
        xre_guard[tt] = (sv_logits_last, ss_sq[tt])
        if tt + 2 < NTT:
            plan_x_load(tt + 2)

    if lim < 3:
        sv_logits_last = None
    wt("s", "SV", sv_logits_last)
    SS_scores = (op("s", lambda eng: eng.activation(scores[:, :], logits[:, :],
                                                    AF.Sigmoid), mark="SS")
                 if lim >= 3 else None)
    wt("v", "SS", SS_scores)
    if lim >= 3:
        def g4(t):
            return t.ap().rearrange("p (g f) -> p g f", f=4)

        def g16(t):
            return t.ap().rearrange("p (g f) -> p g f", f=16)

        def bc4(t):
            return bass.AP(t, 0, [[64, 128], [1, 64], [0, 4]])

        def bc16(t):
            return bass.AP(t, 0, [[16, 128], [1, 16], [0, 16]])

        bias_bcast = bass.AP(bias_bc, 0, [[16, 128], [0, NTT], [1, 16]])
        V_(lambda e: e.tensor_add(
            r_sfc.ap().rearrange("p (t e) -> p t e", e=16),
            scores.ap().rearrange("p (t e) -> p t e", e=16), bias_bcast))
        V_(lambda e: e.reduce_max(g4(r_m64a), g4(r_sfc), axis=AX.X))
        V_(lambda e: e.tensor_tensor(g4(r_eq), g4(r_sfc), bc4(r_m64a),
                                     ALU.is_equal))
        V_(lambda e: e.scalar_tensor_tensor(r_eq[:, :], r_eq[:, :],
                                            -1e30, r_sfc[:, :],
                                            ALU.mult, ALU.add))
        V_(lambda e: e.reduce_max(g4(r_m64b), g4(r_eq), axis=AX.X))
        V_(lambda e: e.tensor_add(r_gs64[:, :], r_m64a[:, :], r_m64b[:, :]))
        V_(lambda e: e.reduce_max(
            r_g16a.ap().rearrange("p (t o) -> p t o", o=1),
            r_gs64.ap().rearrange("p (t g) -> p t g", g=4), axis=AX.X))
        V_(lambda e: e.tensor_tensor(
            r_m64b.ap().rearrange("p (t g) -> p t g", g=4),
            r_gs64.ap().rearrange("p (t g) -> p t g", g=4),
            bass.AP(r_g16a, 0, [[16, 128], [1, 16], [0, 4]]), ALU.is_equal))
        V_(lambda e: e.scalar_tensor_tensor(r_m64b[:, :], r_m64b[:, :],
                                            -1e30, r_gs64[:, :],
                                            ALU.mult, ALU.add))
        V_(lambda e: e.reduce_max(
            r_g16b.ap().rearrange("p (t o) -> p t o", o=1),
            r_m64b.ap().rearrange("p (t g) -> p t g", g=4), axis=AX.X))
        V_(lambda e: e.tensor_tensor(
            r_m64a.ap().rearrange("p (t g) -> p t g", g=4),
            r_gs64.ap().rearrange("p (t g) -> p t g", g=4),
            bass.AP(r_g16b, 0, [[16, 128], [1, 16], [0, 4]]), ALU.is_ge))
        V_(lambda e: e.tensor_tensor(g4(r_tmp), g4(r_sfc), bc4(r_m64a),
                                     ALU.mult))
        V_(lambda e: e.tensor_copy(r_w[:, :], r_tmp[:, :]))
        V_(lambda e: e.reduce_max(
            r_g16a.ap().rearrange("p (t o) -> p t o", o=1), g16(r_w),
            axis=AX.X))
        for _ in range(3):
            V_(lambda e: e.tensor_tensor(g16(r_eq), g16(r_w), bc16(r_g16a),
                                         ALU.is_equal))
            V_(lambda e: e.scalar_tensor_tensor(r_w[:, :], r_eq[:, :],
                                                -1e30, r_w[:, :],
                                                ALU.mult, ALU.add))
            V_(lambda e: e.reduce_max(
                r_g16a.ap().rearrange("p (t o) -> p t o", o=1), g16(r_w),
                axis=AX.X))
        V_(lambda e: e.tensor_tensor(g16(r_eq), g16(r_tmp), bc16(r_g16a),
                                     ALU.is_ge))
        V_(lambda e: e.tensor_mul(r_w[:, :], scores[:, :], r_eq[:, :]))
        V_(lambda e: e.reduce_sum(
            r_g16a.ap().rearrange("p (t o) -> p t o", o=1), g16(r_w),
            axis=AX.X))
        V_(lambda e: e.tensor_scalar_add(r_g16a[:, :], r_g16a[:, :], 1e-20))
        V_(lambda e: e.reciprocal(r_g16b[:, :], r_g16a[:, :]))
        sv_cwall = V_(lambda e: e.tensor_tensor(g16(cw_sb), g16(r_w),
                                                bc16(r_g16b), ALU.mult),
                      mark="SV")
        prev_cp = None
        for tt in range(NTT):
            sl = slice(tt * 16, (tt + 1) * 16)
            wt("t", "SV", sv_cwall)
            if prev_cp:
                wt("t", "SV", prev_cp)
            st_cwt = opt_tr(lambda eng, sl=sl: eng.transpose(
                ps_cwt[:, :], cw_sb[:, sl], identf_sb[:, :]), mark="ST")
            wt("v", "ST", st_cwt)
            prev_cp = V_(lambda e, tt=tt: e.tensor_copy(
                cwT[0:16, tt * 128:(tt + 1) * 128], ps_cwt[:, :]), mark="SV")

    if dbg:
        dump("xdbg", xdbg)
        dump("statall", statall)
        dump("xnb", xnb)
        dump("logits", logits)
        dump("scores", scores)
        dump("cw_sb", cw_sb)
        dump("cwT", cwT)
        dump("W_km", W_km)
        for pt_ in range(2):
            dump(f"Pm_c{pt_}", Pm_c[pt_])
            dump(f"Qm_c{pt_}", Qm_c[pt_])
            dump(f"Pt_c{pt_}", Pt_c[pt_])
            dump(f"Qt_c{pt_}", Qt_c[pt_])
            dump(f"K21_c{pt_}", K21_c[pt_])
            dump(f"K22_c{pt_}", K22_c[pt_])
        for nm_ in PRM:
            dump("prm_" + nm_, PRM[nm_])
        dump("prm_raw", prm)
        dump("graw", graw_sb)
        dump("araw", araw_sb)
        dump("dcol", dcol)
        dump("gwT_sb", gwT_sb)
        dump("bias_bc", bias_bc)
        dump("B_bf", B_bf)
        dump("C_bf", C_bf)
    ps_head.close()
    G_k = sb1("G_k", [128, 4 * 512], F32)
    x2l = sb1("x2l", [128, 4 * L], BF16)
    x2g = x2l
    xl1 = sb1("xl1", [128, 4 * NCH], F32)
    S1_sb = sb1("S1_sb", [128, 4 * (NCH + 1)], F32)
    S2_sb = sb1("S2_sb", [128, 4 * (NCH + 1)], F32)
    ps1 = ExitStack()
    ps_bu = ps1.enter_context(nc.psum_tensor("ps_bu", [128, 512], F32))
    ps_csa = ps1.enter_context(nc.psum_tensor("ps_csa", [128, 512], F32))
    ps_csb = ps1.enter_context(nc.psum_tensor("ps_csb", [128, 512], F32))
    ps_cs1 = ps1.enter_context(nc.psum_tensor("ps_cs1", [128, 8], F32))
    ps_cy = [ps1.enter_context(nc.psum_tensor("ps_cy0", [128, 512], F32))] * 2
    if lim >= 4:
        wt("t", "SWbc", SW_bc)
        if sv_evac[NTT - 1]:
            wt("t", "SV", sv_evac[NTT - 1])

    sv_g = [None] * NCH
    st_bu = [None] * NCH
    st_cs = [None] * NCH
    sv_pm = [None] * NCH
    for ci in range(NCH if lim >= 4 else 0):
        if ci >= 1:
            wt("t", "SV", sv_g[ci - 1])
        for hc in range(HC):
            op("t", lambda eng, ci=ci, hc=hc: eng.matmul(
                ps_bu[:, :],
                xnb[:, hc * L + ci * 128: hc * L + (ci + 1) * 128],
                B_bf[:, hc * 512:(hc + 1) * 512],
                start=(hc == 0), stop=(hc == HC - 1)),
               mark="ST" if hc == HC - 1 else None)
        st_bu[ci] = cnt["ST"]
        wt("v", "ST", st_bu[ci])
        if ci >= 1:
            wt("v", "ST", st_cs[ci - 1])
        for m in range(4):
            V_(lambda e, m=m: e.tensor_mul(
                G_k[:, m * 512:(m + 1) * 512], W_km[:, m * 512:(m + 1) * 512],
                ps_bu[:, :]), mark="SV" if m == 3 else None)
        sv_g[ci] = cnt["SV"]
        wt("t", "SV", sv_g[ci])
        if ci >= 1:
            wt("t", "SV", sv_pm[ci - 1])
        for m, ps_dst in ((2, ps_csa), (3, ps_csb)):
            for ct in range(4):
                op("t", lambda eng, m=m, ct=ct, ps_dst=ps_dst: eng.matmul(
                    ps_dst[:, ct * 128:(ct + 1) * 128],
                    G_k[:, m * 512 + ct * 128: m * 512 + (ct + 1) * 128],
                    tril_sb[:, :], start=True, stop=True))
        for mi, m in enumerate((0, 1)):
            for ct in range(4):
                col = mi * 4 + ct
                op("t", lambda eng, m=m, ct=ct, col=col: eng.matmul(
                    ps_cs1[:, col:col + 1],
                    G_k[:, m * 512 + ct * 128: m * 512 + (ct + 1) * 128],
                    tril_sb[:, 127:128], start=True, stop=True),
                   mark="ST" if col == 7 else None)
        st_cs[ci] = cnt["ST"]
        wt("v", "ST", st_cs[ci])
        for ct in range(4):
            pt = ct % 2
            V_(lambda e, pt=pt, ct=ct: e.tensor_mul(
                dscr[:, 0:128], Pm_c[pt][:, 0:128],
                ps_csa[:, ct * 128:(ct + 1) * 128]))
            V_(lambda e, pt=pt, ct=ct: e.tensor_mul(
                dscr2[:, 0:128], Qm_c[pt][:, 0:128],
                ps_csb[:, ct * 128:(ct + 1) * 128]))
            V_(lambda e, ct=ct, ci=ci: e.tensor_add(
                x2l[:, ct * L + ci * 128: ct * L + (ci + 1) * 128],
                dscr[:, 0:128], dscr2[:, 0:128]))
            V_(lambda e, pt=pt, ct=ct: e.tensor_mul(
                r_col[:, 0:1], Pm_c[pt][:, 127:128], ps_cs1[:, ct:ct + 1]))
            V_(lambda e, pt=pt, ct=ct: e.tensor_mul(
                r_col[:, 1:2], Qm_c[pt][:, 127:128], ps_cs1[:, 4 + ct:5 + ct]))
            V_(lambda e, ct=ct, ci=ci: e.tensor_add(
                xl1[:, ct * NCH + ci: ct * NCH + ci + 1],
                r_col[:, 0:1], r_col[:, 1:2]),
               mark="SV" if ct == 3 else None)
        sv_pm[ci] = cnt["SV"]

    for ct in range(4 if lim >= 4 else 0):
        V_(lambda e, ct=ct: e.memset(
            S1_sb[:, ct * (NCH + 1): ct * (NCH + 1) + 1], 0.0))
        V_(lambda e, ct=ct: e.memset(
            S2_sb[:, ct * (NCH + 1): ct * (NCH + 1) + 1], 0.0))
    for ci in range(NCH if lim >= 4 else 0):
        for ct in range(4):
            pt = ct % 2
            c0 = ct * (NCH + 1) + ci
            xl1c = xl1[:, ct * NCH + ci: ct * NCH + ci + 1]
            xl2c = x2l[:, ct * L + ci * 128 + 127: ct * L + ci * 128 + 128]
            V_(lambda e, pt=pt, c0=c0: e.tensor_scalar_mul(
                r_col[:, 0:1], S1_sb[:, c0:c0 + 1], PRM["mc11"][:, pt:pt + 1]))
            V_(lambda e, pt=pt, c0=c0: e.scalar_tensor_tensor(
                r_col[:, 0:1], S2_sb[:, c0:c0 + 1], PRM["mc12"][:, pt:pt + 1],
                r_col[:, 0:1], ALU.mult, ALU.add))
            V_(lambda e, c0=c0, xl1c=xl1c: e.tensor_add(
                S1_sb[:, c0 + 1:c0 + 2], r_col[:, 0:1], xl1c))
            V_(lambda e, pt=pt, c0=c0: e.tensor_scalar_mul(
                r_col[:, 1:2], S1_sb[:, c0:c0 + 1], PRM["mc21"][:, pt:pt + 1]))
            V_(lambda e, pt=pt, c0=c0: e.scalar_tensor_tensor(
                r_col[:, 1:2], S2_sb[:, c0:c0 + 1], PRM["mc22"][:, pt:pt + 1],
                r_col[:, 1:2], ALU.mult, ALU.add))
            V_(lambda e, c0=c0, xl2c=xl2c: e.tensor_add(
                S2_sb[:, c0 + 1:c0 + 2], r_col[:, 1:2], xl2c))
    sv_x2g = [None] * NCH
    for ci in range(NCH if lim >= 4 else 0):
        for ct in range(4):
            pt = ct % 2
            c0 = ct * (NCH + 1) + ci
            tsl = slice(ct * L + ci * 128, ct * L + (ci + 1) * 128)
            V_(lambda e, pt=pt, c0=c0, tsl=tsl: e.scalar_tensor_tensor(
                dscr[:, 0:128], K21_c[pt][:, :], S1_sb[:, c0:c0 + 1],
                x2l[:, tsl], ALU.mult, ALU.add))
            V_(lambda e, pt=pt, c0=c0, tsl=tsl: e.scalar_tensor_tensor(
                x2g[:, tsl], K22_c[pt][:, :], S2_sb[:, c0:c0 + 1],
                dscr[:, 0:128], ALU.mult, ALU.add),
               mark="SV" if ct == 3 else None)
        sv_x2g[ci] = cnt["SV"]

    sv_sev = []
    for tq in range(4 if lim >= 4 else 0):
        wt("t", "SV", sv_x2g[tq * 4 + 3])
        for hc in range(HC):
            par = (tq * HC + hc) % 2
            if sv_sev:
                wt("t", "SV", sv_sev[-1])
            for ct in range(4):
                op("t", lambda eng, tq=tq, hc=hc, ct=ct, par=par: eng.matmul(
                    ps_cy[par][:, :],
                    C_bf[:, ct * H + hc * 128: ct * H + (hc + 1) * 128],
                    x2g[:, ct * L + tq * 512: ct * L + (tq + 1) * 512],
                    start=(ct == 0), stop=(ct == 3)),
                   mark="ST" if ct == 3 else None)
            st_now = cnt["ST"]
            wt("v", "ST", st_now)
            sv_sev.append(V_(lambda e, tq=tq, hc=hc, par=par: e.scalar_tensor_tensor(
                s_sb[:, hc * L + tq * 512: hc * L + (tq + 1) * 512],
                xnb[:, hc * L + tq * 512: hc * L + (tq + 1) * 512],
                dcol[:, hc:hc + 1], ps_cy[par][:, :], ALU.mult, ALU.add),
                mark="SV"))

    if dbg and lim >= 4:
        dump("x2l", x2l)
        dump("x2g", x2g)
        dump("xl1", xl1)
        dump("S1_sb", S1_sb)
        dump("S2_sb", S2_sb)
        dump("G_k", G_k)
        dump("s_sb", s_sb)
    run_block(p0)
    ps1.close()
    ps0.close()
    es1.close()
    es0.close()
    es01.close()

    y_sb = sbF("y_sb", [128, HC * HALF], F32)
    act_sb = sbF("act_sb", [128, IC * HALF], BF16)
    wg_sb = [sbF(f"wg_sb{i}", [128, H], BF16) for i in range(2)]
    wu_sb = [sbF(f"wu_sb{i}", [128, H], BF16) for i in range(2)]
    wd_sb = sbF("wd_sb", [128, IC * H], BF16)
    sil_t = [sbF(f"sil_t{i}", [128, 512], F32) for i in range(2)]
    ftmp = [sbF(f"ftmp{i}", [128, 512], F32) for i in range(2)]
    x_re = [sbF(f"x_re{i}", [128, H], F32) for i in range(2)]
    out_tok = [sbF(f"out_tok{i}", [128, H], F32) for i in range(2)]
    g_t = [sbF(f"g_t{i}", [128, 128], F32) for i in range(2)]
    sg_t = [sbF(f"sg_t{i}", [128, 128], BF16) for i in range(2)]
    cw_row = [sbF(f"cw_row{i}", [1, HALF], F32) for i in range(2)]
    cwbdbg = sbF("cwbdbg", [128, HALF], F32)
    gu_consumed = {}
    wld = {}
    for half in range(2 if lim >= 5 else 0):
        plan = new_plan()
        pF = plan
        psF = ExitStack()
        ps_g = [psF.enter_context(nc.psum_tensor(f"ps_g{half}_{i}", [128, 512], F32))
                for i in range(2)]
        ps_u = [psF.enter_context(nc.psum_tensor(f"ps_u{half}_{i}", [128, 512], F32))
                for i in range(2)]
        ps_y = [psF.enter_context(nc.psum_tensor(f"ps_y{half}_{i}", [128, 512], F32))
                for i in range(2)]
        ps_cwb = psF.enter_context(nc.psum_tensor(f"ps_cwb{half}", [128, 1024], F32))

        act_sv = {}
        dn_st = {}
        ye_sv = {}
        cwbmm_st = {}
        for e_i in range(nexlim):
            shared = e_i >= E

            def chunk_gidx(ic):
                return (half * NEX + e_i) * IC + ic

            def plan_wload(ic, e_i=None, shared=None):
                gidx = chunk_gidx(ic)
                bpar = gidx % 2
                if gidx - 2 in gu_consumed:
                    wt("g", "ST", gu_consumed[gidx - 2])
                for w_i in range(2):
                    buf = (wg_sb if w_i == 0 else wu_sb)[bpar]
                    s_nm = (f"SWg{bpar}" if w_i == 0 else f"SWu{bpar}")
                    if shared:
                        sic = (e_i - E) * IC + ic
                        dram = sgT_d if w_i == 0 else suT_d
                        src_ap = dram.ap()[sic].rearrange(
                            "(hcp p) i -> p hcp i", p=128)
                    else:
                        dram = wgT_d if w_i == 0 else wuT_d
                        src_ap = dram.ap()[e_i, ic].rearrange(
                            "(hcp p) i -> p hcp i", p=128)
                    op("g", lambda eng, buf=buf, src_ap=src_ap: eng.dma_start(
                        out=buf.ap().rearrange("p (hcp i) -> p hcp i", hcp=HC),
                        in_=src_ap), mark=s_nm, inc=16)
                    wld[(gidx, w_i)] = (s_nm, cnt[s_nm])

            plan_wload(0, e_i=e_i, shared=shared)
            plan_wload(1, e_i=e_i, shared=shared)
            for ic in range(IC):
                if (e_i - 1, 15) in dn_st:
                    wt("g", "ST", dn_st[(e_i - 1, 15)])
                if shared:
                    r0 = (e_i - E) * I_E + ic * 128
                    src_ap = sdT_d[r0:r0 + 128, :]
                else:
                    src_ap = wdT_d.ap()[e_i, ic * 128:(ic + 1) * 128, :]
                op("g", lambda eng, ic=ic, src_ap=src_ap: eng.dma_start(
                    out=wd_sb[:, ic * H:(ic + 1) * H], in_=src_ap),
                   mark="SWd", inc=16)
            wdld_swd = cnt["SWd"]

            if not shared:
                rpar = e_i % 2
                rsem = "SCa" if rpar == 0 else "SCb"
                if (e_i - 2) in cwbmm_st:
                    wt("y", "ST", cwbmm_st[e_i - 2])
                op("y", lambda eng, e_i=e_i, rpar=rpar, half=half:
                   eng.dma_start(out=cw_row[rpar][:, :],
                                 in_=cwT[e_i:e_i + 1,
                                         half * HALF:(half + 1) * HALF]),
                   mark=rsem, inc=16)
                sx_row = cnt[rsem]

                def plan_cwb(rpar=rpar, rsem=rsem, sx_row=sx_row, e_i=e_i):
                    if (e_i - 1, 21) in act_sv:
                        wt("t", "SV", act_sv[(e_i - 1, 21)])
                    wt("t", rsem, sx_row)
                    for piece in range(2):
                        op("t", lambda eng, rpar=rpar, piece=piece: eng.matmul(
                            ps_cwb[:, piece * 512:(piece + 1) * 512],
                            ones_sb[:, :],
                            cw_row[rpar][:, piece * 512:(piece + 1) * 512],
                            start=True, stop=True),
                           mark="ST" if piece == 1 else None)
                    cwbmm_st[e_i] = cnt["ST"]
            else:
                plan_cwb = None
                if dbg and e_i == 0 and half == 1:
                    wt("v", "ST", cwbmm_st[e_i])
                    V_(lambda e: e.tensor_copy(cwbdbg[:, :], ps_cwb[:, :]),
                       mark="SV")

            for ic in range(IC):
                gidx = chunk_gidx(ic)
                bpar = gidx % 2
                wt("t", *wld[(gidx, 0)])
                wt("t", *wld[(gidx, 1)])
                for ts in range(2):
                    k = ic * 2 + ts
                    a_key = (e_i, k - 2) if k >= 2 else (e_i - 1, 20 + k)
                    if a_key in act_sv:
                        wt("t", "SV", act_sv[a_key])
                    for w_i in range(2):
                        wbuf = (wg_sb if w_i == 0 else wu_sb)[bpar]
                        ps_dst = (ps_g if w_i == 0 else ps_u)[k % 2]
                        for hc in range(HC):
                            op("t", lambda eng, wbuf=wbuf, ps_dst=ps_dst,
                               hc=hc, ts=ts, half=half: eng.matmul(
                                   ps_dst[:, :],
                                   wbuf[:, hc * 128:(hc + 1) * 128],
                                   xnb[:, hc * L + half * HALF + ts * 512:
                                       hc * L + half * HALF + (ts + 1) * 512],
                                   start=(hc == 0), stop=(hc == HC - 1)),
                               mark="ST" if (w_i == 1 and hc == HC - 1) else None)
                    gu_consumed[gidx] = cnt["ST"]
                    st_k = cnt["ST"]
                    if ic == 0 and ts == 0 and plan_cwb is not None:
                        plan_cwb()
                    if ts == 1 and ic + 2 < IC:
                        plan_wload(ic + 2, e_i=e_i, shared=shared)
                    wt("s", "ST", st_k)
                    if (e_i, k - 2) in act_sv:
                        wt("s", "SV", act_sv[(e_i, k - 2)])
                    elif (e_i - 1, 20 + k) in act_sv:
                        wt("s", "SV", act_sv[(e_i - 1, 20 + k)])
                    ss_k = op("s", lambda eng, k=k: eng.activation(
                        sil_t[k % 2][:, :], ps_g[k % 2][:, :], AF.Silu),
                        mark="SS")
                    wt("v", "SS", ss_k)
                    asl = slice(ic * HALF + ts * 512, ic * HALF + (ts + 1) * 512)
                    if shared:
                        act_sv[(e_i, k)] = V_(
                            lambda e, k=k, asl=asl: e.tensor_mul(
                                act_sb[:, asl], sil_t[k % 2][:, :],
                                ps_u[k % 2][:, :]), mark="SV")
                    else:
                        if ic == 0 and e_i in cwbmm_st:
                            wt("v", "ST", cwbmm_st[e_i])
                        V_(lambda e, k=k: e.tensor_mul(
                            ftmp[k % 2][:, :], sil_t[k % 2][:, :],
                            ps_u[k % 2][:, :]))
                        act_sv[(e_i, k)] = V_(
                            lambda e, k=k, ts=ts, asl=asl: e.tensor_mul(
                                act_sb[:, asl], ftmp[k % 2][:, :],
                                ps_cwb[:, ts * 512:(ts + 1) * 512]), mark="SV")

            wt("t", "SWd", wdld_swd)
            for gi in range(16):
                ts, hc = gi // 8, gi % 8
                if gi == 0 or gi == 8:
                    wt("t", "SV", act_sv[(e_i, 20 + ts)])
                y_key = (e_i, gi - 2) if gi >= 2 else (e_i - 1, 14 + gi)
                if y_key in ye_sv:
                    wt("t", "SV", ye_sv[y_key])
                for ic in range(IC):
                    op("t", lambda eng, ic=ic, hc=hc, ts=ts, gi=gi: eng.matmul(
                        ps_y[gi % 2][:, :],
                        wd_sb[:, ic * H + hc * 128: ic * H + (hc + 1) * 128],
                        act_sb[:, ic * HALF + ts * 512:
                               ic * HALF + (ts + 1) * 512],
                        start=(ic == 0), stop=(ic == IC - 1)),
                       mark="ST" if ic == IC - 1 else None)
                dn_st[(e_i, gi)] = cnt["ST"]
                wt("v", "ST", dn_st[(e_i, gi)])
                ysl = slice(hc * HALF + ts * 512, hc * HALF + (ts + 1) * 512)
                if e_i == 0:
                    ye_sv[(e_i, gi)] = V_(lambda e, gi=gi, ysl=ysl: e.tensor_copy(
                        y_sb[:, ysl], ps_y[gi % 2][:, :]), mark="SV")
                else:
                    ye_sv[(e_i, gi)] = V_(lambda e, gi=gi, ysl=ysl: e.tensor_add(
                        y_sb[:, ysl], y_sb[:, ysl], ps_y[gi % 2][:, :]),
                        mark="SV")

        run_block(pF)
        psF.close()

        plan = new_plan()
        pT = plan
        psT2 = ExitStack()
        ps_o = [psT2.enter_context(nc.psum_tensor(f"ps_o{half}_{i}", [128, H], BF16))
                for i in range(2)]
        sv_sg = {}
        st_tr = {}
        sv_ot = {}
        so_out = {}
        so_xre = {}
        for tt8 in range(8):
            par = tt8 % 2
            rsem = "SRa" if par == 0 else "SRb"
            osem = "SOa" if par == 0 else "SOb"
            if (tt8 - 2) in sv_ot:
                wt("y", "SV", sv_ot[tt8 - 2])
            t0 = half * HALF + tt8 * 128
            op("y", lambda eng, par=par, t0=t0: eng.dma_start(
                out=x_re[par][:, :], in_=x_d[t0:t0 + 128, :]),
               mark=rsem, inc=16)
            so_xre[tt8] = (rsem, cnt[rsem])
            for hc in range(HC):
                kk = tt8 * HC + hc
                p2 = kk % 2
                if (kk - 2) in sv_sg:
                    wt("s", "SV", sv_sg[kk - 2])
                ysl = slice(hc * HALF + tt8 * 128, hc * HALF + (tt8 + 1) * 128)
                ss_g = op("s", lambda eng, p2=p2, ysl=ysl: eng.activation(
                    g_t[p2][:, :], y_sb[:, ysl], AF.Silu), mark="SS")
                wt("v", "SS", ss_g)
                if (kk - 2) in st_tr:
                    wt("v", "ST", st_tr[kk - 2])
                ssl = slice(hc * L + half * HALF + tt8 * 128,
                            hc * L + half * HALF + (tt8 + 1) * 128)
                sv_sg[kk] = V_(lambda e, p2=p2, ssl=ssl: e.tensor_mul(
                    sg_t[p2][:, :], s_sb[:, ssl], g_t[p2][:, :]), mark="SV")
                wt("t", "SV", sv_sg[kk])
                if (tt8 - 2) in sv_ot and hc == 0:
                    wt("t", "SV", sv_ot[tt8 - 2])
                st_tr[kk] = opt_tr(lambda eng, p2=p2, par=par, hc=hc:
                                   eng.transpose(ps_o[par][:, hc * 128:(hc + 1) * 128],
                                                 sg_t[p2][:, :], identb_sb[:, :]),
                                   mark="ST")
            wt("v", "ST", st_tr[tt8 * HC + HC - 1])
            wt("v", *so_xre[tt8])
            if (tt8 - 2) in so_out:
                wt("v", *so_out[tt8 - 2])
            sv_ot[tt8] = V_(lambda e, par=par: e.tensor_add(
                out_tok[par][:, :], x_re[par][:, :], ps_o[par][:, :]),
                mark="SV")
            wt("y", "SV", sv_ot[tt8])
            op("y", lambda eng, par=par, t0=t0: eng.dma_start(
                out=out_d[t0:t0 + 128, :], in_=out_tok[par][:, :]),
               mark=osem, inc=16)
            so_out[tt8] = (osem, cnt[osem])
        if dbg:
            dump(f"y_sb{half}", y_sb)
            if half == 1:
                dump("act_sb", act_sb)
                dump("cwbdbg", cwbdbg)
                dump("cwrow_dbg", cw_row[0])
        if half == 1:
            wt("y", "SOa", cnt["SOa"])
            wt("y", "SOb", cnt["SOb"])
        run_block(pT)
        psT2.close()

    if lim < 5:
        plan = new_plan()
        wt("y", "SOa", cnt["SOa"])
        run_block(plan)
    esF.close()
    es.close()
    return nc


_BUILT = None


def _get_nc():
    global _BUILT
    if _BUILT is None:
        _BUILT = build_nc()
    return _BUILT


def kernel(L0_attn_input_sequence, L0_attn_steps_raw, L0_attn_G_raw,
           L0_attn_A_raw, L0_attn_B_mat, L0_attn_C_mat, L0_attn_D_vec,
           L0_ffn_gate_w, L0_ffn_gate_bias, L0_ffn_gate_proj, L0_ffn_up_proj,
           L0_ffn_down_proj, L0_ffn_shared_gate_w, L0_ffn_shared_up_w,
           L0_ffn_shared_down_w):
    f32 = np.float32
    x = np.ascontiguousarray(np.asarray(L0_attn_input_sequence, f32))
    B_mat = np.asarray(L0_attn_B_mat, f32)
    C_mat = np.asarray(L0_attn_C_mat, f32)
    bT = np.ascontiguousarray(B_mat.transpose(1, 2, 0).reshape(H, 2 * P))
    cT = np.ascontiguousarray(C_mat.transpose(2, 1, 0).reshape(2 * P, H))
    gw = np.ascontiguousarray(np.asarray(L0_ffn_gate_w, f32))
    gbias = np.ascontiguousarray(np.asarray(L0_ffn_gate_bias, f32).reshape(1, 16))
    wgT = np.ascontiguousarray(
        np.asarray(L0_ffn_gate_proj, f32).transpose(0, 2, 1)
        .reshape(E, H, IC, 128).transpose(0, 2, 1, 3))
    wuT = np.ascontiguousarray(
        np.asarray(L0_ffn_up_proj, f32).transpose(0, 2, 1)
        .reshape(E, H, IC, 128).transpose(0, 2, 1, 3))
    wdT = np.ascontiguousarray(np.asarray(L0_ffn_down_proj, f32).transpose(0, 2, 1))
    sgT = np.ascontiguousarray(
        np.asarray(L0_ffn_shared_gate_w, f32).T
        .reshape(H, 2 * IC, 128).transpose(1, 0, 2))
    suT = np.ascontiguousarray(
        np.asarray(L0_ffn_shared_up_w, f32).T
        .reshape(H, 2 * IC, 128).transpose(1, 0, 2))
    sdT = np.ascontiguousarray(np.asarray(L0_ffn_shared_down_w, f32).T)
    tril = np.triu(np.ones((128, 128), f32))
    identf = np.eye(128, dtype=f32)
    identb = np.eye(128, dtype=f32).astype(ml_dtypes.bfloat16)
    onesr = np.ones((1, 128), f32)

    common = dict(
        bT=bT, cT=cT, gw=gw, gbias=gbias,
        dvec=np.ascontiguousarray(np.asarray(L0_attn_D_vec, f32)),
        steps_raw=np.ascontiguousarray(np.asarray(L0_attn_steps_raw, f32)),
        g_raw=np.ascontiguousarray(np.asarray(L0_attn_G_raw, f32)),
        a_raw=np.ascontiguousarray(np.asarray(L0_attn_A_raw, f32)),
        wgT=wgT, wuT=wuT, wdT=wdT, sgT=sgT, suT=suT, sdT=sdT,
        tril=tril, identf=identf, identb=identb, onesr=onesr,
    )
    in_maps = [dict(common, x=np.ascontiguousarray(x[b])) for b in range(8)]
    nc = _get_nc()
    res = run_bass_kernel_spmd(nc, in_maps, core_ids=list(range(8)))
    out = np.stack([np.asarray(r["out"], f32) for r in res.results], axis=0)
    return out
